# version 19
# baseline (speedup 1.0000x reference)
import contextlib
import numpy as np

import concourse.bass as bass
import concourse.mybir as mybir
from concourse.bass_utils import run_bass_kernel_spmd

F32 = mybir.dt.float32
BF16 = mybir.dt.bfloat16
AF = mybir.ActivationFunctionType
ALU = mybir.AluOpType
AX = mybir.AxisListType

L = 2048
NQ = 4
DM = 1024
NEXP = 6
EPS = 1e-5

SB_BASE = 16512
SB_LIMIT = 229376

ENGINES = ("sy", "te", "sc", "ve", "ge")


class Builder:
    def __init__(self, nc):
        self.nc = nc
        self.cmds = {e: [] for e in ENGINES}
        self.cnt = {s: 0 for s in ("dw", "do", "te", "sc", "ve", "ge", "sy")}
        self.sems = {}

    def sb(self, name, cols, dtype, offset):
        itemsize = 2 if dtype == BF16 else 4
        assert offset + cols * itemsize <= SB_LIMIT, (name, offset, cols)
        return self.nc.alloc_sbuf_tensor_at(name, [128, cols], dtype, offset=offset).ap()

    def emit(self, e, fn):
        self.cmds[e].append(fn)

    def sig(self, e, make, sem, k=1):
        self.cnt[sem] += k
        tgt = self.cnt[sem]
        self.emit(e, lambda E, make=make, s=sem, k=k: make(E).then_inc(self.sems[s], k))
        return tgt

    def run(self, e, make):
        self.emit(e, lambda E, make=make: make(E))

    def wait(self, e, sem, val):
        if val > 0:
            self.emit(e, lambda E, sem=sem, val=val: E.wait_ge(self.sems[sem], val))

    def nop_sig(self, e, sem):
        return self.sig(e, lambda E: E.drain(), sem)

    def barrier(self, engines=("sy", "te", "sc", "ve")):
        vals = {e: self.nop_sig(e, e) for e in engines}
        for e in engines:
            for o in engines:
                if o != e:
                    self.wait(e, o, vals[o])

    def dma_w(self, out, in_):
        return self.sig("sy", lambda E, o=out, i=in_: E.dma_start(out=o, in_=i), "dw", 16)

    def dma_o(self, out, in_):
        return self.sig("sc", lambda E, o=out, i=in_: E.dma_start(out=o, in_=i), "do", 16)


def build_nc(debug_taps=(), upto="full", skip_ffn=False):
    nc = bass.Bass("TRN2", target_bir_lowering=False, detect_race_conditions=False)
    b = Builder(nc)

    def dram(n, s, ty, kind):
        return nc.dram_tensor(n, list(s), ty, kind=kind).ap()
    x_fm = dram("x_fm", (8, 128, L), F32, "ExternalInput")
    w_in = dram("w_in", (32, 128, 1024), F32, "ExternalInput")
    w_bc = dram("w_bc", (128, 512), F32, "ExternalInput")
    w_xpd = dram("w_xpd", (16, 128, 2048), F32, "ExternalInput")
    w_dtT = dram("w_dtT", (16, 128, 2048), F32, "ExternalInput")
    conv_w_d = dram("conv_w", (128, 64), F32, "ExternalInput")
    conv_b_d = dram("conv_b", (128, 16), F32, "ExternalInput")
    a_log_d = dram("a_log", (128, 256), F32, "ExternalInput")
    d_par_d = dram("d_par", (128, 16), F32, "ExternalInput")
    dt_b_d = dram("dt_b", (128, 16), F32, "ExternalInput")
    w_out = dram("w_out", (8, 128, 2048), F32, "ExternalInput")
    gate_w_d = dram("gate_w", (128, 32), F32, "ExternalInput")
    gate_b_d = dram("gate_b", (1, 4), F32, "ExternalInput")
    fw1 = dram("fw1", (NEXP, 32, 128, 1024), F32, "ExternalInput")
    fb1_d = dram("fb1", (128, NEXP * 32), F32, "ExternalInput")
    fw2 = dram("fw2", (NEXP, 8, 128, 4096), F32, "ExternalInput")
    fb2_d = dram("fb2", (128, NEXP * 8), F32, "ExternalInput")
    y_fm = dram("y_fm", (8, 128, L), F32, "ExternalOutput")
    mt2 = dram("mt2", (16, 128, 2048), BF16, "Internal")
    wxp_bfd = dram("wxp_bfd", (16, 128, 2048), BF16, "Internal")
    w_in_bf = dram("w_in_bf", (32, 128, 1024), BF16, "Internal")
    w_out_bf = dram("w_out_bf", (8, 128, 2048), BF16, "Internal")
    fw1_bf = dram("fw1_bf", (NEXP, 32, 128, 1024), BF16, "Internal")
    fw2_bf = dram("fw2_bf", (NEXP, 8, 128, 4096), BF16, "Internal")

    O = SB_BASE
    xh_t = b.sb("xh", 8 * 1024, F32, O); O += 32768

    def xslot(q):
        p = q % 2
        return [xh_t[:, k * 1024 + p * 512: k * 1024 + p * 512 + 512] for k in range(8)]
    xn_t = b.sb("xn", 8 * 1024, BF16, O); O += 16384

    def xnhalf(q):
        p = q % 2
        return [xn_t[:, k * 1024 + p * 512: k * 1024 + p * 512 + 512] for k in range(8)]

    def rwbc_for(q):
        p = (q + 1) % 2
        return [xn_t[:, e * 1024 + p * 512: e * 1024 + p * 512 + 512] for e in range(4)]

    O_BIGB = O
    bigb = b.sb("bigb", 32 * 1024, BF16, O_BIGB); O += 65536
    xi_act = [bigb[:, i * 512:(i + 1) * 512] for i in range(16)]
    sz = [bigb[:, (16 + i) * 512:(17 + i) * 512] for i in range(16)]
    HB = 16384
    hbuf = [bigb[:, HB + i * 512: HB + (i + 1) * 512] for i in range(32)]
    O_H2 = O_BIGB + 32768
    lnrow_m = b.sb("lnrow_m", 512, F32, O_H2)[0:1, :]
    lnrow_q = b.sb("lnrow_q", 512, F32, O_H2 + 2048)[0:1, :]
    lnrow_r = b.sb("lnrow_r", 512, F32, O_H2 + 4096)[0:1, :]
    rstd_bc = b.sb("rstd_bc", 512, F32, O_H2 + 6144)
    mr_bc = b.sb("mr_bc", 512, F32, O_H2 + 8192)
    lntmp = b.sb("lntmp", 512, F32, O_H2 + 10240)
    sqtmp = b.sb("sqtmp", 512, F32, O_H2 + 12288)
    cvt_f32 = [b.sb(f"cvt_f32_{i}", 2048, F32, O_H2 + i * 8192) for i in range(3)]
    cvt_bf = [b.sb(f"cvt_bf{i}", 2048, BF16, O_H2 + 24576 + i * 4096) for i in range(2)]
    cdtT = b.sb("cdtT", 16 * 512, F32, O_H2)
    O_GATED = O_BIGB + 65536
    gated_t = b.sb("gated", 8 * 1024, BF16, O_GATED)
    gated = [gated_t[:, i * 512:(i + 1) * 512] for i in range(16)]
    w2bf = [b.sb(f"w2bf{i}", 2048, BF16, O_GATED + 16384 + i * 4096) for i in range(2)]
    delta_r = [b.sb(f"delta{i}", 512, F32, O_GATED + 24576 + i * 2048) for i in range(2)]
    r0_b = b.sb("r0", 512, F32, O_GATED + 28672)
    r1_b = b.sb("r1", 512, F32, O_GATED + 30720)
    wxpf32 = [b.sb(f"wxpf32_{i}", 2048, F32, O_GATED + i * 8192) for i in range(2)]
    wxp_bf = [b.sb(f"wxp_bf{i}", 2048, BF16, O_GATED + 16384 + i * 4096) for i in range(2)]
    O_BBC = O_GATED + 32768
    bbc_t = b.sb("bbc", 32 * 512, BF16, O_BBC)
    bbc = [bbc_t[:, i * 512:(i + 1) * 512] for i in range(32)]
    cdtT_bf = b.sb("cdtT_bf", 16 * 512, BF16, O_BBC)
    O_WORK = O_BBC + 32768
    dx_b = b.sb("dx", 512, F32, O_WORK)
    BC_sb = b.sb("BC_sb", 512, BF16, O_WORK + 2048)[0:32, :]
    dA_b = b.sb("dA", 512, F32, O_WORK + 4096)
    vtmp = b.sb("vtmp", 512, F32, O_WORK + 6144)
    H_b = b.sb("H", 512, F32, O_WORK + 8192)
    y_acc = b.sb("y_acc", 512, F32, O_WORK + 10240)
    O_CONV = O_WORK + 12288
    xi_pre = b.sb("xi_pre", 528, F32, O_CONV)
    cacc = [b.sb(f"cacc{i}", 512, F32, O_CONV + 2112 + i * 2048) for i in range(2)]
    obf = b.sb("obf", 2048, BF16, O_CONV)
    wring = [b.sb(f"wring{i}", 2048, BF16, o) for i, o in enumerate(
        (O_CONV, O_CONV + 4096, O_GATED + 16384, O_GATED + 20480))]
    rwT = b.sb("rwT", 512, F32, O_CONV)[0:4, :]
    ffntmp = b.sb("ffntmp", 512, F32, O_WORK + 2048)
    w1bf = [b.sb(f"w1bf{i}", 1024, BF16, O_CONV + 6208 + i * 2048) for i in range(2)]
    dA1_b = b.sb("dA1", 512, F32, O_CONV + 8256)
    w4f32 = [b.sb(f"w4f32_{i}", 1024, F32, O_GATED + 24576 + i * 4096) for i in range(2)]
    wbf_in = [b.sb(f"wbf_in{i}", 1024, BF16, O_CONV + 6208 + i * 2048) for i in range(2)]
    O_MT = O_CONV + 10368
    mtb = b.sb("mtb", 2048, BF16, O_MT)
    iota_rbuf = b.sb("iota_rbuf", 128, F32, O_CONV + 2112)
    off = O_MT + 4096

    def small(name, cols, ty):
        nonlocal off
        ap = b.sb(name, cols, ty, off)
        itemsize = 2 if ty == BF16 else 4
        off = (off + cols * itemsize + 31) & ~31
        return ap

    A_sb = small("A_sb", 256, F32)
    conv_w = small("conv_w_s", 64, F32)
    conv_b = small("conv_b_s", 16, F32)
    d_par = small("d_par_s", 16, F32)
    dt_b = small("dt_b_s", 16, F32)
    state = small("state", 256, F32)
    carry = small("carry", 48, F32)
    fb1 = small("fb1_s", NEXP * 32, F32)
    fb2 = small("fb2_s", NEXP * 8, F32)
    fb2h = small("fb2h", 16, F32)
    gw_bf = small("gw_bf", 32, BF16)
    gb_f = small("gb_f", 4, F32)[0:1, :]
    gb_bf = small("gb_bf", 4, BF16)[0:1, :]
    wbc_bf = small("wbc_bf", 512, BF16)
    ones_col = small("ones_col", 1, F32)
    eps_col = small("eps_col", 1, F32)
    onesr_bf = small("onesr_bf", 128, BF16)[0:1, :]
    iota_c = small("iota_c", 1, F32)
    ident = small("ident", 128, F32)
    identbf = small("identbf", 32, BF16)
    onesr_f = ident[0:1, 0:1].broadcast_to([1, 128])
    assert off <= SB_LIMIT, off
    g16 = b.sb("g16", 16, F32, O_CONV + 4160)
    e16 = b.sb("e16", 16, F32, O_CONV + 4224)
    w16 = b.sb("w16", 16, F32, O_CONV + 4288)
    wm16 = b.sb("wm16", 16, F32, O_CONV + 4352)
    msk16 = b.sb("msk16", 16, F32, O_CONV + 4416)
    mx4 = b.sb("mx4", 4, F32, O_CONV + 4480)
    s4c = b.sb("s4c", 4, F32, O_CONV + 4512)
    m1c = b.sb("m1c", 4, F32, O_CONV + 4544)
    m2c = b.sb("m2c", 4, F32, O_CONV + 4576)
    gwf_st = b.sb("gwf_st", 32, F32, O_CONV + 4608)

    pb = [nc.alloc_psum_tensor(f"pb{i}", [128, 512], F32).ap() for i in range(8)]

    es = contextlib.ExitStack()
    for s in ("dw", "do", "te", "sc", "ve", "ge", "sy"):
        b.sems[s] = es.enter_context(nc.semaphore(s))

    for dst, src in ((conv_w, conv_w_d), (conv_b, conv_b_d), (A_sb, a_log_d),
                     (d_par, d_par_d), (dt_b, dt_b_d), (fb1, fb1_d), (fb2, fb2_d),
                     (gwf_st, gate_w_d), (gb_f, gate_b_d), (wbc_bf, w_bc)):
        if dst is wbc_bf:
            b.dma_w(cvt_f32[0][:, 0:512], src)
        else:
            b.dma_w(dst, src)
    dw_smalls = b.cnt["dw"]
    for ap, v in ((ones_col, 1.0), (eps_col, EPS), (onesr_bf, 1.0),
                  (state, 0.0)):
        b.run("ve", lambda E, ap=ap, v=v: E.memset(ap, v))
    b.run("ge", lambda E: E.iota(iota_c, [[0, 1]], base=0, channel_multiplier=1,
                                 allow_small_or_imprecise_dtypes=True))
    gdone = b.sig("ge", lambda E: E.iota(iota_rbuf, [[1, 128]], base=0, channel_multiplier=0,
                                         allow_small_or_imprecise_dtypes=True), "ge")
    b.wait("ve", "ge", gdone)
    b.run("ve", lambda E: E.tensor_scalar(out=ident, in0=iota_rbuf, scalar1=iota_c,
                                          scalar2=None, op0=ALU.is_equal))
    b.wait("sc", "dw", dw_smalls)
    exp_done = b.sig("sc", lambda E: E.activation(out=A_sb, in_=A_sb, func=AF.Exp,
                                                  bias=0.0, scale=1.0), "sc")
    b.wait("ve", "sc", exp_done)
    b.run("ve", lambda E: E.tensor_scalar_mul(out=A_sb, in0=A_sb, scalar1=-1.0))
    b.run("sc", lambda E: E.activation(out=wbc_bf, in_=cvt_f32[0][:, 0:512],
                                       func=AF.Identity, bias=0.0, scale=1.0))
    b.run("sc", lambda E: E.activation(out=gw_bf, in_=gwf_st, func=AF.Identity, bias=0.0, scale=1.0))
    b.run("sc", lambda E: E.activation(out=gb_bf, in_=gb_f, func=AF.Identity, bias=0.0, scale=1.0))
    b.run("sc", lambda E: E.activation(out=fb2h, in_=fb2[:, 0:16], func=AF.Identity, bias=0.0, scale=0.5))
    b.wait("sc", "ve", b.cnt["ve"])
    b.run("sc", lambda E: E.activation(out=identbf[0:32, :], in_=ident[0:32, 0:32],
                                       func=AF.Identity, bias=0.0, scale=1.0))
    b.barrier(("sy", "te", "sc", "ve", "ge"))

    gated_sig = {}
    scan_done = {0 - 1: 0}
    store_tgt = {}
    prep_sc_done = {}
    prep_te_done = {}
    prep_xn_done = {}
    tail_sc_done = {-1: 0, 0 - 2: 0}
    tail_te_done = {}
    tail_ve_done = {}
    dl_done = {}
    scansig = {}
    spill = {"w_in": 0, "w_out": 0, "ffn": 0}

    def emit_composition_gen():
        b.wait("sy", "ve", prep_xn_done[0])
        wxp_cast = {}
        comp_mm = {}
        comp_ev = {}
        scat = {}
        wsp = {}
        pend = []

        def flush_scat():
            while pend:
                gg, oc_, it_, stage_ = pend.pop(0)
                b.wait("sy", "sc", comp_ev[gg])
                for j in range(4):
                    od = oc_ * 4 + j
                    scat[gg] = b.dma_w(mt2[od, :, it_ * 128:(it_ + 1) * 128],
                                       stage_[:, j * 128:(j + 1) * 128])
        g = 0
        for oc in range(4):
            flush_scat()
            dl = None
            for jb in range(16):
                dl = b.dma_w(cdtT[:, jb * 512:(jb + 1) * 512],
                             w_dtT[jb, :, oc * 512:(oc + 1) * 512])
            b.wait("sc", "dw", dl)
            for jb in range(16):
                b.run("sc", lambda E, jb=jb: E.activation(
                    out=cdtT_bf[:, jb * 512:(jb + 1) * 512],
                    in_=cdtT[:, jb * 512:(jb + 1) * 512], func=AF.Identity, bias=0.0, scale=1.0))
            b.nop_sig("sc", "sc")
            yield
            for it in range(16):
                slot = g % 2
                if oc == 0:
                    if g >= 2:
                        b.wait("sy", "sc", wxp_cast[g - 2])
                    d = b.dma_w(wxpf32[slot], w_xpd[it])
                    flush_scat()
                    b.wait("sc", "dw", d)
                    if g >= 2:
                        b.wait("sc", "do", wsp[g - 2])
                        b.wait("sc", "dw", scat[g - 2])
                    wxp_cast[g] = b.sig("sc", lambda E, slot=slot: E.activation(
                        out=wxp_bf[slot], in_=wxpf32[slot], func=AF.Identity,
                        bias=0.0, scale=1.0), "sc")
                    b.wait("sc", "sc", wxp_cast[g])
                    wsp[g] = b.dma_o(wxp_bfd[it], wxp_bf[slot])
                    b.wait("te", "sc", wxp_cast[g])
                else:
                    if g >= 2:
                        b.wait("sy", "te", comp_mm[g - 2])
                    d = b.dma_w(wxp_bf[slot], wxp_bfd[it])
                    flush_scat()
                    b.wait("te", "dw", d)
                bank = 4 + (g % 2)
                if g >= 2:
                    b.wait("te", "sc", comp_ev[g - 2])
                for jb in range(16):
                    mk = (lambda E, slot=slot, jb=jb, bank=bank, s=(jb == 0), p=(jb == 15):
                          E.matmul(out=pb[bank], lhsT=wxp_bf[slot][:, jb * 128:(jb + 1) * 128],
                                   rhs=cdtT_bf[:, jb * 512:(jb + 1) * 512], start=s, stop=p))
                    if jb == 15:
                        comp_mm[g] = b.sig("te", mk, "te")
                    else:
                        b.run("te", mk)
                b.wait("sc", "te", comp_mm[g])
                stage = mtb[:, 0:1024] if g % 2 == 0 else mtb[:, 1024:2048]
                if g >= 2:
                    b.wait("sc", "dw", scat[g - 2])
                comp_ev[g] = b.sig("sc", lambda E, bank=bank, stage=stage: E.activation(
                    out=stage[:, 0:512], in_=pb[bank], func=AF.Identity, bias=0.0, scale=1.0), "sc")
                pend.append((g, oc, it, stage))
                g += 1
                yield
        flush_scat()

    def emit_ln(xs, xns):
        for k in range(8):
            b.run("te", lambda E, k=k, s=(k == 0), p=(k == 7):
                  E.matmul(out=pb[0][0:1, :], lhsT=ones_col, rhs=xs[k], start=s, stop=p))
        mm_tgt = {}
        for k in range(8):
            if k >= 1:
                b.wait("sc", "te", mm_tgt[k - 1])
            sq = b.sig("sc", lambda E, k=k: E.activation(
                out=sqtmp, in_=xs[k], func=AF.Square, bias=0.0, scale=1.0), "sc")
            b.wait("te", "sc", sq)
            mm_tgt[k] = b.sig("te", lambda E, k=k, s=(k == 0), p=(k == 7):
                              E.matmul(out=pb[1][0:1, :], lhsT=ones_col,
                                       rhs=sqtmp, start=s, stop=p), "te")
        b.wait("ve", "te", mm_tgt[7])
        b.run("ve", lambda E: E.tensor_scalar_mul(out=lnrow_m, in0=pb[0][0:1, :], scalar1=1.0 / DM))
        b.run("ve", lambda E: E.tensor_scalar_mul(out=lnrow_q, in0=pb[1][0:1, :], scalar1=1.0 / DM))
        b.run("ve", lambda E: E.tensor_mul(out=lnrow_r, in0=lnrow_m, in1=lnrow_m))
        rows = b.sig("ve", lambda E: E.tensor_sub(out=lnrow_q, in0=lnrow_q, in1=lnrow_r), "ve")
        b.wait("sc", "ve", rows)
        rstd = b.sig("sc", lambda E: E.activation(out=lnrow_q, in_=lnrow_q, func=AF.Sqrt,
                                                  bias=eps_col[0:1, :], scale=1.0), "sc")
        b.wait("ve", "sc", rstd)
        b.run("ve", lambda E: E.reciprocal(out=lnrow_q, in_=lnrow_q))
        b.run("ve", lambda E: E.drain())
        mr = b.sig("ve", lambda E: E.tensor_mul(out=lnrow_r, in0=lnrow_m, in1=lnrow_q), "ve")
        b.wait("te", "ve", mr)
        b.run("te", lambda E: E.matmul(out=pb[2], lhsT=onesr_f, rhs=lnrow_q, start=True, stop=True))
        bc_mm = b.sig("te", lambda E: E.matmul(out=pb[3], lhsT=onesr_f, rhs=lnrow_r,
                                               start=True, stop=True), "te")
        b.wait("sc", "te", bc_mm)
        b.run("sc", lambda E: E.activation(out=rstd_bc, in_=pb[2], func=AF.Identity, bias=0.0, scale=1.0))
        ev = b.sig("sc", lambda E: E.activation(out=mr_bc, in_=pb[3], func=AF.Identity,
                                                bias=0.0, scale=1.0), "sc")
        b.wait("ve", "sc", ev)
        b.wait("te", "sc", ev)
        for k in range(8):
            b.run("ve", lambda E, k=k: E.tensor_mul(out=lntmp, in0=xs[k], in1=rstd_bc))
            b.run("ve", lambda E, k=k, xns=xns: E.tensor_sub(out=xns[k], in0=lntmp, in1=mr_bc))
        return b.nop_sig("ve", "ve")

    def emit_prep(q, comp_gen=None):
        xs = xslot(q)
        xns = xnhalf(q)
        if q >= 2:
            b.wait("sy", "do", store_tgt[q - 2])
        if q == 1:
            b.wait("sy", "dw", spill["w_in"])
        if q >= 1:
            b.wait("sy", "te", b.cnt["te"])
            b.wait("sy", "ve", scan_done.get(q - 1, 0))
        b.wait("te", "sc", tail_sc_done.get(q - 2, 0))
        b.wait("te", "ve", tail_ve_done.get(q - 2, 0))
        b.wait("sc", "te", tail_te_done.get(q - 2, 0))
        xd = None
        for k in range(8):
            xd = b.dma_w(xs[k], x_fm[k, :, q * 512:(q + 1) * 512])
        for e in ("te", "sc", "ve"):
            b.wait(e, "dw", xd)
        prep_xn_done[q] = emit_ln(xs, xns)
        b.wait("te", "ve", prep_xn_done[q])

        cast_tgt = {}
        evac_sc = {}
        for mt in range(32):
            slot = mt % 2
            if q == 0:
                if mt >= 2:
                    b.wait("sy", "sc", cast_tgt[mt - 2])
                d = b.dma_w(w4f32[slot], w_in[mt])
                if mt >= 1:
                    b.wait("sy", "sc", cast_tgt[mt - 1])
                    inspill[mt - 1] = b.dma_w(w_in_bf[mt - 1], wbf_in[(mt - 1) % 2])
                b.wait("sc", "dw", d)
                if mt >= 2:
                    b.wait("sc", "dw", inspill[mt - 2])
                    b.wait("sc", "te", mm_prev[mt - 2])
                cast_tgt[mt] = b.sig("sc", lambda E, slot=slot: E.activation(
                    out=wbf_in[slot], in_=w4f32[slot], func=AF.Identity, bias=0.0, scale=1.0), "sc")
                b.wait("te", "sc", cast_tgt[mt])
            else:
                if mt >= 2:
                    b.wait("sy", "te", mm_prev[mt - 2])
                d = b.dma_w(wbf_in[slot], w_in_bf[mt])
                b.wait("te", "dw", d)
            bank = mt % 4
            if mt >= 4:
                b.wait("te", "sc", evac_sc[mt - 4])
            for k in range(8):
                mk = (lambda E, slot=slot, k=k, bank=bank, s=(k == 0), p=(k == 7):
                      E.matmul(out=pb[bank], lhsT=wbf_in[slot][:, k * 128:(k + 1) * 128],
                               rhs=xns[k], start=s, stop=p))
                if k == 7:
                    mm = b.sig("te", mk, "te")
                else:
                    b.run("te", mk)
            mm_prev[mt] = mm
            b.wait("sc", "te", mm)
            if mt < 16:
                dblk = mt
                if q >= 1:
                    b.wait("sc", "ve", gated_sig[(q - 1, dblk)])
                b.run("sc", lambda E, bank=bank: E.activation(
                    out=xi_pre[:, 3:515], in_=pb[bank], func=AF.Identity, bias=0.0, scale=1.0))
                if q == 0:
                    b.run("sc", lambda E: E.memzero(xi_pre[:, 0:3]))
                else:
                    b.run("sc", lambda E, dblk=dblk: E.activation(
                        out=xi_pre[:, 0:3], in_=carry[:, dblk * 3:(dblk + 1) * 3],
                        func=AF.Identity, bias=0.0, scale=1.0))
                b.run("sc", lambda E, dblk=dblk: E.activation(
                    out=carry[:, dblk * 3:(dblk + 1) * 3], in_=xi_pre[:, 512:515],
                    func=AF.Identity, bias=0.0, scale=1.0))
                tap0 = b.sig("sc", lambda E, dblk=dblk: E.activation(
                    out=cacc[0], in_=xi_pre[:, 0:512], func=AF.Identity,
                    bias=conv_b[:, dblk:dblk + 1], scale=conv_w[:, dblk * 4:dblk * 4 + 1]), "sc")
                b.wait("ve", "sc", tap0)
                for t in range(1, 3):
                    b.run("ve", lambda E, dblk=dblk, t=t: E.scalar_tensor_tensor(
                        out=cacc[t % 2], in0=xi_pre[:, t:t + 512],
                        scalar=conv_w[:, dblk * 4 + t:dblk * 4 + t + 1],
                        in1=cacc[(t - 1) % 2], op0=ALU.mult, op1=ALU.add))
                taps = b.sig("ve", lambda E, dblk=dblk: E.scalar_tensor_tensor(
                    out=cacc[1], in0=xi_pre[:, 3:515],
                    scalar=conv_w[:, dblk * 4 + 3:dblk * 4 + 4],
                    in1=cacc[0], op0=ALU.mult, op1=ALU.add), "ve")
                b.wait("sc", "ve", taps)
                evac_sc[mt] = b.sig("sc", lambda E, dblk=dblk: E.activation(
                    out=xi_act[dblk], in_=cacc[1], func=AF.Silu, bias=0.0, scale=1.0), "sc")
            else:
                dblk = mt - 16
                if q >= 1:
                    b.wait("sc", "ve", gated_sig[(q - 1, dblk)])
                evac_sc[mt] = b.sig("sc", lambda E, bank=bank, dblk=dblk: E.activation(
                    out=sz[dblk], in_=pb[bank], func=AF.Silu, bias=0.0, scale=1.0), "sc")
            if comp_gen is not None:
                next(comp_gen, None)
                next(comp_gen, None)
        if q == 0:
            b.wait("sy", "sc", cast_tgt[31])
            inspill[31] = b.dma_w(w_in_bf[31], wbf_in[1])
            spill["w_in"] = inspill[31]
        if comp_gen is not None:
            for _ in comp_gen:
                pass
            b.wait("te", "sc", b.cnt["sc"])
            b.wait("sy", "te", b.cnt["te"])
            b.wait("sy", "sc", b.cnt["sc"])
            b.wait("sy", "do", b.cnt["do"])
        b.wait("te", "sc", evac_sc[31])
        for k in range(16):
            mk = (lambda E, k=k, s=(k == 0), p=(k == 15):
                  E.matmul(out=pb[4][0:32, :], lhsT=wbc_bf[:, k * 32:(k + 1) * 32],
                           rhs=xi_act[k], start=s, stop=p))
            if k == 15:
                mm = b.sig("te", mk, "te")
            else:
                b.run("te", mk)
        b.wait("sc", "te", mm)
        bc_ev = b.sig("sc", lambda E: E.activation(
            out=BC_sb, in_=pb[4][0:32, :], func=AF.Identity, bias=0.0, scale=1.0), "sc")
        b.wait("te", "sc", bc_ev)
        prep_sc_done[q] = b.cnt["sc"]
        prep_te_done[q] = b.cnt["te"]

    inspill = {}
    mm_prev = {}

    def emit_bcasts(q):
        b.wait("sc", "ve", scan_done[q - 1])
        b.wait("te", "ve", scan_done[q - 1])
        evs = {}
        for r in range(32):
            bank = 2 + (r % 6)
            if r >= 6:
                b.wait("te", "sc", evs[r - 6])
            sel = identbf[0:32, r:r + 1].broadcast_to([32, 128])
            mm = b.sig("te", lambda E, sel=sel, bank=bank: E.matmul(
                out=pb[bank], lhsT=sel, rhs=BC_sb, start=True, stop=True), "te")
            b.wait("sc", "te", mm)
            evs[r] = b.sig("sc", lambda E, r=r, bank=bank: E.activation(
                out=bbc[r], in_=pb[bank], func=AF.Identity, bias=0.0, scale=1.0), "sc")
        b.wait("te", "sc", evs[31])
        b.wait("te", "sc", evs[30])

    def emit_m_od(q, od, prev_mm=0):
        b.wait("sy", "te", prev_mm)
        d = b.dma_w(mtb, mt2[od, :, :])
        b.wait("te", "dw", d)
        bank = 6 + (od % 2)
        for ib in range(16):
            mk = (lambda E, ib=ib, bank=bank, s=(ib == 0), p=(ib == 15):
                  E.matmul(out=pb[bank], lhsT=mtb[:, ib * 128:(ib + 1) * 128],
                           rhs=xi_act[ib], start=s, stop=p))
            if ib == 15:
                mm = b.sig("te", mk, "te")
            else:
                b.run("te", mk)
        return mm

    def emit_softplus(q, od, m_mm):
        if od >= 1:
            b.wait("sc", "ve", dl_done[(q, od - 1)])
        b.wait("sc", "te", m_mm)
        bank = 6 + (od % 2)
        b.run("sc", lambda E, bank=bank, od=od: E.activation(
            out=r0_b, in_=pb[bank], func=AF.Identity,
            bias=dt_b[:, od:od + 1], scale=1.0))
        b.run("sc", lambda E: E.activation(out=r1_b, in_=r0_b, func=AF.Abs, bias=0.0, scale=1.0))
        b.run("sc", lambda E: E.activation(out=r1_b, in_=r1_b, func=AF.Exp, bias=0.0, scale=-1.0))
        return b.sig("sc", lambda E: E.activation(out=r1_b, in_=r1_b, func=AF.Ln,
                                                  bias=ones_col, scale=1.0), "sc")

    def emit_scan(q, tail_gen, aux_gen=None):
        def pump(k):
            if tail_gen is not None:
                for _ in range(k):
                    if next(tail_gen, "END") == "END":
                        break
        def pump_aux(k):
            if aux_gen is not None:
                for _ in range(k):
                    if next(aux_gen, "END") == "END":
                        break
        m_mm = {}
        sp_sig = {}
        b.wait("sc", "te", prep_te_done.get(q, 0))
        if q == 0:
            b.wait("sc", "dw", spill["w_in"])
        m_mm[0] = emit_m_od(q, 0)
        sp_sig[0] = emit_softplus(q, 0, m_mm[0])
        m_mm[1] = emit_m_od(q, 1)
        for od in range(16):
            dblk = od
            ds = od % 2
            dlt = delta_r[ds]
            sp = sp_sig[od]
            b.wait("ve", "sc", sp)
            b.run("ve", lambda E, dlt=dlt: E.scalar_tensor_tensor(
                out=dlt, in0=r0_b, scalar=0.0, in1=r1_b, op0=ALU.max, op1=ALU.add))
            dl = b.sig("ve", lambda E, dlt=dlt, dblk=dblk: E.tensor_mul(
                out=dx_b, in0=dlt, in1=xi_act[dblk]), "ve")
            dl_done[(q, od)] = dl
            if od + 1 < 16:
                sp_sig[od + 1] = emit_softplus(q, od + 1, m_mm[od + 1])
            if od + 2 < 16:
                b.wait("te", "sc", sp_sig[od])
                m_mm[od + 2] = emit_m_od(q, od + 2, m_mm[od + 1])
            post = None
            save_state = q < NQ - 1
            for n in range(16):
                col = dblk * 16 + n
                gn = q * 256 + dblk * 16 + n
                dAs = dA_b if n % 2 == 0 else dA1_b
                b.wait("sc", "ve", scansig.get(gn - 2, 0))
                da = b.sig("sc", lambda E, dlt=dlt, col=col, dAs=dAs: E.activation(
                    out=dAs, in_=dlt, func=AF.Exp, bias=0.0,
                    scale=A_sb[:, col:col + 1]), "sc")
                b.run("ve", lambda E, n=n: E.tensor_mul(out=vtmp, in0=dx_b, in1=bbc[n]))
                b.wait("ve", "sc", da)
                scansig[gn] = b.sig("ve", lambda E, col=col, dAs=dAs: E.tensor_tensor_scan(
                    out=H_b, data0=dAs, data1=vtmp,
                    initial=state[:, col:col + 1], op0=ALU.mult, op1=ALU.add), "ve")
                if save_state:
                    b.run("ve", lambda E, col=col: E.tensor_copy(
                        out=state[:, col:col + 1], in_=H_b[:, 511:512]))
                if n == 0:
                    post = b.sig("ve", lambda E, n=n: E.tensor_mul(
                        out=y_acc, in0=H_b, in1=bbc[16 + n]), "ve")
                else:
                    b.run("ve", lambda E, n=n: E.tensor_mul(out=vtmp, in0=H_b, in1=bbc[16 + n]))
                    post = b.sig("ve", lambda E: E.tensor_add(
                        out=y_acc, in0=y_acc, in1=vtmp), "ve")
                if n % 2 == 1:
                    pump(1)
                pump_aux(1)
            b.run("ve", lambda E, dblk=dblk: E.scalar_tensor_tensor(
                out=y_acc, in0=xi_act[dblk], scalar=d_par[:, dblk:dblk + 1],
                in1=y_acc, op0=ALU.mult, op1=ALU.add))
            if q >= 1 and dblk == 0:
                while (q - 1) not in outproj_mm_last:
                    pump(1)
                b.wait("ve", "te", outproj_mm_last[q - 1])
            gated_sig[(q, dblk)] = b.sig("ve", lambda E, dblk=dblk: E.tensor_mul(
                out=gated[dblk], in0=y_acc, in1=sz[dblk]), "ve")
            pump(2)
            pump_aux(4)
        scan_done[q] = b.cnt["ve"]
        if tail_gen is not None:
            for _ in tail_gen:
                pass
        if aux_gen is not None:
            for _ in aux_gen:
                pass

    outproj_mm_last = {}

    def emit_conversion_gen():
        jobs = [("w_out", w_out[mo], w_out_bf[mo], 2048, 1.0) for mo in range(8)]
        for e in range(NEXP):
            for mt in range(32):
                jobs.append(("ffn", fw1[e, mt], fw1_bf[e, mt], 1024, 1.0))
        for e in range(NEXP):
            sc2 = 0.5 if e < 2 else 1.0
            for mo in range(8):
                jobs.append(("ffn", fw2[e, mo, :, 0:2048], fw2_bf[e, mo, :, 0:2048], 2048, sc2))
                jobs.append(("ffn", fw2[e, mo, :, 2048:4096], fw2_bf[e, mo, :, 2048:4096], 2048, sc2))
        casts = {}
        spills = {}
        for j, (kind, src, dst, cols, scl) in enumerate(jobs):
            slot = j % 3
            bslot = j % 2
            if j >= 3:
                b.wait("sy", "sc", casts[j - 3])
            d = b.dma_w(cvt_f32[slot][:, 0:cols], src)
            b.wait("sc", "dw", d)
            if j >= 2:
                b.wait("sc", "do", spills[j - 2])
            casts[j] = b.sig("sc", lambda E, slot=slot, bslot=bslot, cols=cols, scl=scl: E.activation(
                out=cvt_bf[bslot][:, 0:cols], in_=cvt_f32[slot][:, 0:cols],
                func=AF.Identity, bias=0.0, scale=scl), "sc")
            b.wait("sc", "sc", casts[j])
            spills[j] = b.dma_o(dst, cvt_bf[bslot][:, 0:cols])
            spill[kind] = spills[j]
            yield

    def emit_tail_gen(q):
        xs = xslot(q)
        xns = xnhalf(q)
        rwbc = rwbc_for(q)
        b.wait("te", "ve", scan_done.get(q, 0))
        b.wait("te", "sc", prep_sc_done.get(q + 1, 0))
        b.wait("te", "sc", tail_sc_done.get(q - 1, 0))
        b.wait("te", "ve", prep_xn_done.get(q + 1, 0))
        if q == 0:
            b.wait("sy", "do", spill["w_out"])
        b.wait("sy", "sc", prep_sc_done.get(q + 1, 0))
        b.wait("sy", "te", max(prep_te_done.get(q + 1, 0), tail_te_done.get(q - 1, 0)))
        b.wait("sy", "ve", tail_ve_done.get(q - 1, 0))
        op_evs = {}
        mm_last = {}
        ring_user = {}
        for mo in range(8):
            slot = mo % 4
            if slot in ring_user:
                b.wait("sy", "te", ring_user[slot])
            d = b.dma_w(wring[slot], w_out_bf[mo])
            if mo >= 1:
                b.wait("ve", "te", mm_last[mo - 1])
                op_evs[mo - 1] = b.sig("ve", lambda E, mo=mo: E.tensor_add(
                    out=xs[mo - 1], in0=pb[(mo - 1) % 4], in1=xs[mo - 1]), "ve")
            b.wait("te", "dw", d)
            bank = mo % 4
            if mo >= 4:
                b.wait("te", "ve", op_evs[mo - 4])
            for k in range(16):
                mk = (lambda E, slot=slot, k=k, bank=bank, s=(k == 0), p=(k == 15):
                      E.matmul(out=pb[bank], lhsT=wring[slot][:, k * 128:(k + 1) * 128],
                               rhs=gated[k], start=s, stop=p))
                if k == 15:
                    mm = b.sig("te", mk, "te")
                else:
                    b.run("te", mk)
            mm_last[mo] = mm
            ring_user[slot] = mm
            yield
        b.wait("ve", "te", mm_last[7])
        op_evs[7] = b.sig("ve", lambda E: E.tensor_add(
            out=xs[7], in0=pb[3], in1=xs[7]), "ve")
        outproj_mm_last[q] = mm_last[7]
        b.wait("te", "ve", op_evs[7])
        b.wait("sc", "ve", op_evs[7])
        xn_done = emit_ln(xs, xns)
        yield
        b.wait("te", "ve", xn_done)
        for tg in range(4):
            for k in range(8):
                b.run("te", lambda E, tg=tg, k=k, s=(k == 0):
                      E.matmul(out=pb[0][:, tg * 4:(tg + 1) * 4],
                               lhsT=xns[k][:, tg * 128:(tg + 1) * 128],
                               rhs=gw_bf[:, k * 4:(k + 1) * 4], start=s, stop=False))
            mk = (lambda E, tg=tg: E.matmul(
                out=pb[0][:, tg * 4:(tg + 1) * 4], lhsT=onesr_bf, rhs=gb_bf,
                start=False, stop=True))
            if tg == 3:
                mm = b.sig("te", mk, "te")
            else:
                b.run("te", mk)
        b.wait("sc", "te", mm)
        evg = b.sig("sc", lambda E: E.activation(
            out=g16, in_=pb[0][:, 0:16], func=AF.Identity, bias=0.0, scale=1.0), "sc")
        b.wait("ve", "sc", evg)

        def rv(mk):
            b.run("ve", mk)
            b.run("ve", lambda E: E.drain())
        g3 = g16.rearrange('p (a b) -> p a b', a=4)
        e3 = e16.rearrange('p (a b) -> p a b', a=4)
        w3 = w16.rearrange('p (a b) -> p a b', a=4)
        wm3 = wm16.rearrange('p (a b) -> p a b', a=4)
        mk3 = msk16.rearrange('p (a b) -> p a b', a=4)
        bc4 = lambda a: a.unsqueeze(-1).broadcast_to([128, 4, 4])
        rv(lambda E: E.reduce_max(out=mx4, in_=g3, axis=AX.X))
        rv(lambda E: E.tensor_sub(out=e3, in0=g3, in1=bc4(mx4)))
        mx = b.nop_sig("ve", "ve")
        b.wait("sc", "ve", mx)
        ex = b.sig("sc", lambda E: E.activation(
            out=e16, in_=e16, func=AF.Exp, bias=0.0, scale=1.0), "sc")
        b.wait("ve", "sc", ex)
        rv(lambda E: E.reduce_sum(out=s4c, in_=e3, axis=AX.X))
        rv(lambda E: E.reciprocal(out=s4c, in_=s4c))
        rv(lambda E: E.tensor_mul(out=w3, in0=e3, in1=bc4(s4c)))
        rv(lambda E: E.reduce_max(out=m1c, in_=w3, axis=AX.X))
        rv(lambda E: E.tensor_tensor(out=mk3, in0=w3, in1=bc4(m1c), op=ALU.is_ge))
        rv(lambda E: E.scalar_tensor_tensor(out=wm3, in0=mk3, scalar=-1e30,
                                            in1=w3, op0=ALU.mult, op1=ALU.add))
        rv(lambda E: E.reduce_max(out=m2c, in_=wm3, axis=AX.X))
        rv(lambda E: E.tensor_tensor(out=mk3, in0=w3, in1=bc4(m2c), op=ALU.is_ge))
        rv(lambda E: E.tensor_mul(out=w3, in0=w3, in1=mk3))
        rw = b.nop_sig("ve", "ve")
        b.wait("te", "ve", rw)
        yield
        evt = {}
        for tg in range(4):
            bank2 = 2 + (tg % 2)
            if tg >= 2:
                b.wait("te", "sc", evt[tg - 2])
            tr = b.sig("te", lambda E, tg=tg, bank2=bank2: E.transpose(
                out=pb[bank2][0:4, 0:128], in_=w16[:, tg * 4:(tg + 1) * 4],
                identity=ident), "te")
            b.wait("sc", "te", tr)
            evt[tg] = b.sig("sc", lambda E, tg=tg, bank2=bank2: E.activation(
                out=rwT[:, tg * 128:(tg + 1) * 128], in_=pb[bank2][0:4, 0:128],
                func=AF.Identity, bias=0.0, scale=1.0), "sc")
        b.wait("te", "sc", evt[3])
        b.wait("te", "sc", evt[2])
        b.wait("ve", "sc", evt[3])
        yield
        revs = {}
        for e4 in range(4):
            bank = 4 + (e4 % 2)
            if e4 >= 2:
                b.wait("te", "sc", revs[e4 - 2])
            sel = ident[0:4, e4:e4 + 1].broadcast_to([4, 128])
            mm = b.sig("te", lambda E, sel=sel, bank=bank: E.matmul(
                out=pb[bank], lhsT=sel, rhs=rwT, start=True, stop=True), "te")
            b.wait("sc", "te", mm)
            revs[e4] = b.sig("sc", lambda E, e4=e4, bank=bank, rwbc=rwbc: E.activation(
                out=rwbc[e4], in_=pb[bank], func=AF.Identity, bias=0.0, scale=1.0), "sc")
        b.wait("te", "sc", revs[3])
        yield
        if skip_ffn:
            b.wait("sc", "ve", op_evs[7])
            st = None
            for k in range(8):
                st = b.dma_o(y_fm[k, :, q * 512:(q + 1) * 512], xs[k])
            store_tgt[q] = st
            tail_sc_done[q] = b.cnt["sc"]
            tail_te_done[q] = b.cnt["te"]
            tail_ve_done[q] = b.cnt["ve"]
            yield
            return
        if q == 0:
            b.wait("sy", "do", spill["ffn"])
        b.wait("sy", "te", b.cnt["te"])
        dn_ev = {}
        ring_user = {}
        rslot = [0]

        def ring_load(srcs):
            slot = rslot[0] % 4
            rslot[0] += 1
            if slot in ring_user:
                b.wait("sy", "te", ring_user[slot])
            n = 2048 // len(srcs)
            d = None
            for i, src in enumerate(srcs):
                d = b.dma_w(wring[slot][:, i * n:(i + 1) * n], src)
            return slot, d

        def up_evac(e, idx):
            b.wait("sc", "te", up_mm[idx])
            up_ev[idx] = b.sig("sc", lambda E, e=e, idx=idx: E.activation(
                out=hbuf[idx], in_=pb[idx % 2], func=AF.Relu,
                bias=fb1[:, e * 32 + idx:e * 32 + idx + 1], scale=1.0), "sc")

        for e in range(NEXP):
            up_mm = {}
            up_ev = {}
            for mt2_ in range(16):
                mt = 2 * mt2_
                slot, d = ring_load([fw1_bf[e, mt], fw1_bf[e, mt + 1]])
                if mt2_ >= 1:
                    up_evac(e, mt - 2)
                    up_evac(e, mt - 1)
                b.wait("te", "dw", d)
                for j in range(2):
                    bank = (mt + j) % 2
                    if mt + j >= 2:
                        b.wait("te", "sc", up_ev[mt + j - 2])
                    for k in range(8):
                        mk = (lambda E, slot=slot, j=j, k=k, bank=bank, s=(k == 0), p=(k == 7):
                              E.matmul(out=pb[bank],
                                       lhsT=wring[slot][:, j * 1024 + k * 128: j * 1024 + (k + 1) * 128],
                                       rhs=xns[k], start=s, stop=p))
                        if k == 7:
                            mm = b.sig("te", mk, "te")
                        else:
                            b.run("te", mk)
                    up_mm[mt + j] = mm
                ring_user[slot] = up_mm[mt + 1]
                yield
            up_evac(e, 30)
            up_evac(e, 31)
            b.wait("te", "sc", up_ev[31])
            def dn_evac(e, mo):
                bank = 2 + (mo % 4)
                b.wait("ve", "te", dn_mm[2 * mo + 1])
                if e < 2:
                    ev = b.sig("ve", lambda E, bank=bank, e=e, mo=mo: E.scalar_tensor_tensor(
                        out=xs[mo], in0=pb[bank],
                        scalar=fb2h[:, e * 8 + mo:e * 8 + mo + 1],
                        in1=xs[mo], op0=ALU.add, op1=ALU.add), "ve")
                else:
                    b.run("ve", lambda E, bank=bank, e=e, mo=mo, rwbc=rwbc: E.scalar_tensor_tensor(
                        out=ffntmp, in0=pb[bank],
                        scalar=fb2[:, e * 8 + mo:e * 8 + mo + 1],
                        in1=rwbc[e - 2], op0=ALU.add, op1=ALU.mult))
                    ev = b.sig("ve", lambda E, mo=mo: E.tensor_add(
                        out=xs[mo], in0=xs[mo], in1=ffntmp), "ve")
                dn_ev[(e, mo)] = ev

            dn_mm = {}
            for mo in range(8):
                for half in range(2):
                    g2 = 2 * mo + half
                    slot, d = ring_load([fw2_bf[e, mo, :, half * 2048:(half + 1) * 2048]])
                    if half == 0 and mo >= 1:
                        dn_evac(e, mo - 1)
                    b.wait("te", "dw", d)
                    bank = 2 + (mo % 4)
                    if half == 0 and mo >= 4:
                        b.wait("te", "ve", dn_ev[(e, mo - 4)])
                    for k in range(16):
                        mk = (lambda E, slot=slot, k=k, bank=bank, half=half,
                              s=(half == 0 and k == 0), p=(half == 1 and k == 15):
                              E.matmul(out=pb[bank], lhsT=wring[slot][:, k * 128:(k + 1) * 128],
                                       rhs=hbuf[half * 16 + k], start=s, stop=p))
                        if k == 15:
                            mm = b.sig("te", mk, "te")
                        else:
                            b.run("te", mk)
                    dn_mm[g2] = mm
                    ring_user[slot] = mm
                yield
            dn_evac(e, 7)
        b.wait("sc", "ve", dn_ev[(NEXP - 1, 7)])
        st = None
        for k in range(8):
            st = b.dma_o(y_fm[k, :, q * 512:(q + 1) * 512], xs[k])
        store_tgt[q] = st
        tail_sc_done[q] = b.cnt["sc"]
        tail_te_done[q] = b.cnt["te"]
        tail_ve_done[q] = b.cnt["ve"]
        yield

    for q in range(NQ + 1):
        if q < NQ:
            emit_prep(q, comp_gen=emit_composition_gen() if q == 0 else None)
            emit_bcasts(q)
        tail_gen = emit_tail_gen(q - 1) if q >= 1 else None
        if q < NQ:
            emit_scan(q, tail_gen, emit_conversion_gen() if q == 0 else None)
        else:
            if tail_gen is not None:
                for _ in tail_gen:
                    pass

    tapmap = {"xn": xn_t, "bigb": bigb, "BCfull": BC_sb, "gated": gated_t,
              "xh": xh_t, "state": state, "bbcT": bbc_t}
    out_tgts = [store_tgt[qq] for qq in store_tgt]
    for name in debug_taps:
        ap = tapmap[name]
        tap = nc.dram_tensor(name, list(ap.shape), ap.dtype, kind="ExternalOutput").ap()
        b.barrier()
        out_tgts.append(b.dma_o(tap, ap))

    if out_tgts:
        b.wait("sy", "do", max(out_tgts))
    b.wait("sy", "dw", b.cnt["dw"])

    with nc.Block() as blk:
        @blk.sync
        def _(E):
            for fn in b.cmds["sy"]:
                fn(E)

        @blk.tensor
        def _(E):
            for fn in b.cmds["te"]:
                fn(E)

        @blk.scalar
        def _(E):
            for fn in b.cmds["sc"]:
                fn(E)

        @blk.vector
        def _(E):
            for fn in b.cmds["ve"]:
                fn(E)

        @blk.gpsimd
        def _(E):
            for fn in b.cmds["ge"]:
                fn(E)

    es.close()
    return nc, b


def prep_inputs(L0_attn_x, L0_attn_in_proj_w, L0_attn_conv_w, L0_attn_conv_b,
                L0_attn_x_proj_w, L0_attn_dt_proj_w, L0_attn_dt_proj_b, L0_attn_A_log,
                L0_attn_D_param, L0_attn_out_proj_w, L0_ffn_gate_w, L0_ffn_gate_b,
                L0_ffn_sw1, L0_ffn_sb1, L0_ffn_sw2, L0_ffn_sb2, L0_ffn_rw1, L0_ffn_rb1,
                L0_ffn_rw2, L0_ffn_rb2):
    f = np.ascontiguousarray
    W1 = np.asarray(L0_attn_in_proj_w, np.float32)
    w_in = f(W1.reshape(32, 128, 8, 128).transpose(0, 3, 2, 1)).reshape(32, 128, 1024)
    WBC = np.asarray(L0_attn_x_proj_w, np.float32)[2048:2080]
    w_bc = f(WBC.reshape(32, 16, 128).transpose(2, 1, 0)).reshape(128, 512)
    WXD = np.asarray(L0_attn_x_proj_w, np.float32)[0:2048]
    w_xpd = f(WXD.reshape(16, 128, 16, 128).transpose(2, 1, 0, 3)).reshape(16, 128, 2048)
    w_dtT = f(np.asarray(L0_attn_dt_proj_w, np.float32).T).reshape(16, 128, 2048)
    conv_w = f(np.asarray(L0_attn_conv_w, np.float32).reshape(16, 128, 4)
               .transpose(1, 0, 2)).reshape(128, 64)
    conv_b = f(np.asarray(L0_attn_conv_b, np.float32).reshape(16, 128).T)
    a_log = f(np.asarray(L0_attn_A_log, np.float32).reshape(16, 128, 16)
              .transpose(1, 0, 2)).reshape(128, 256)
    d_par = f(np.asarray(L0_attn_D_param, np.float32).reshape(16, 128).T)
    dt_b = f(np.asarray(L0_attn_dt_proj_b, np.float32).reshape(16, 128).T)
    WO = np.asarray(L0_attn_out_proj_w, np.float32)
    w_out = f(WO.reshape(8, 128, 16, 128).transpose(0, 3, 2, 1)).reshape(8, 128, 2048)
    gate_w = f(np.asarray(L0_ffn_gate_w, np.float32).reshape(8, 128, 4)
               .transpose(1, 0, 2)).reshape(128, 32)
    gate_b = np.asarray(L0_ffn_gate_b, np.float32).reshape(1, 4)

    def up_bundle(w):
        return f(w.reshape(8, 128, 32, 128).transpose(2, 1, 0, 3)).reshape(32, 128, 1024)

    def down_bundle(w):
        return f(w.reshape(32, 128, 8, 128).transpose(2, 1, 0, 3)).reshape(8, 128, 4096)

    sw1 = np.asarray(L0_ffn_sw1, np.float32)
    rw1 = np.asarray(L0_ffn_rw1, np.float32)
    sw2 = np.asarray(L0_ffn_sw2, np.float32)
    rw2 = np.asarray(L0_ffn_rw2, np.float32)
    fw1 = np.stack([up_bundle(sw1[0]), up_bundle(sw1[1])] +
                   [up_bundle(rw1[i]) for i in range(4)])
    fw2 = np.stack([down_bundle(sw2[0]), down_bundle(sw2[1])] +
                   [down_bundle(rw2[i]) for i in range(4)])
    sb1 = np.asarray(L0_ffn_sb1, np.float32)
    rb1 = np.asarray(L0_ffn_rb1, np.float32)
    sb2 = np.asarray(L0_ffn_sb2, np.float32)
    rb2 = np.asarray(L0_ffn_rb2, np.float32)
    fb1 = f(np.concatenate([sb1, rb1], 0).reshape(6, 32, 128)
            .transpose(2, 0, 1)).reshape(128, 192)
    fb2 = f(np.concatenate([sb2, rb2], 0).reshape(6, 8, 128)
            .transpose(2, 0, 1)).reshape(128, 48)

    shared = dict(w_in=w_in, w_bc=w_bc, w_xpd=w_xpd, w_dtT=w_dtT, conv_w=conv_w,
                  conv_b=conv_b, a_log=a_log, d_par=d_par, dt_b=dt_b, w_out=w_out,
                  gate_w=gate_w, gate_b=gate_b, fw1=fw1, fb1=fb1, fw2=fw2, fb2=fb2)

    x = np.asarray(L0_attn_x, np.float32)
    in_maps = []
    for bi in range(8):
        x_fm_b = f(x[bi].T).reshape(8, 128, 2048)
        in_maps.append(dict(shared, x_fm=x_fm_b))
    return in_maps



def kernel(**inputs):
    in_maps = prep_inputs(**inputs)
    nc, _ = build_nc()
    res = run_bass_kernel_spmd(nc, in_maps, core_ids=list(range(8)))
    out = np.empty((8, 2048, 1024), np.float32)
    for bi in range(8):
        out[bi] = res.results[bi]["y_fm"].reshape(1024, 2048).T
    return out


# version 21 (final)
# speedup vs baseline: 1.0540x; 1.0540x over previous
import contextlib
import numpy as np

import concourse.bass as bass
import concourse.mybir as mybir
from concourse.bass_utils import run_bass_kernel_spmd

F32 = mybir.dt.float32
BF16 = mybir.dt.bfloat16
AF = mybir.ActivationFunctionType
ALU = mybir.AluOpType
AX = mybir.AxisListType

L = 2048
NQ = 4
DM = 1024
NEXP = 6
EPS = 1e-5

SB_BASE = 16512
SB_LIMIT = 229376

ENGINES = ("sy", "te", "sc", "ve", "ge")


class Builder:
    def __init__(self, nc):
        self.nc = nc
        self.cmds = {e: [] for e in ENGINES}
        self.cnt = {s: 0 for s in ("dw", "do", "te", "sc", "ve", "ge", "sy")}
        self.sems = {}

    def sb(self, name, cols, dtype, offset):
        itemsize = 2 if dtype == BF16 else 4
        assert offset + cols * itemsize <= SB_LIMIT, (name, offset, cols)
        return self.nc.alloc_sbuf_tensor_at(name, [128, cols], dtype, offset=offset).ap()

    def emit(self, e, fn):
        self.cmds[e].append(fn)

    def sig(self, e, make, sem, k=1):
        self.cnt[sem] += k
        tgt = self.cnt[sem]
        self.emit(e, lambda E, make=make, s=sem, k=k: make(E).then_inc(self.sems[s], k))
        return tgt

    def run(self, e, make):
        self.emit(e, lambda E, make=make: make(E))

    def wait(self, e, sem, val):
        if val > 0:
            self.emit(e, lambda E, sem=sem, val=val: E.wait_ge(self.sems[sem], val))

    def nop_sig(self, e, sem):
        return self.sig(e, lambda E: E.drain(), sem)

    def barrier(self, engines=("sy", "te", "sc", "ve")):
        vals = {e: self.nop_sig(e, e) for e in engines}
        for e in engines:
            for o in engines:
                if o != e:
                    self.wait(e, o, vals[o])

    def dma_w(self, out, in_):
        return self.sig("sy", lambda E, o=out, i=in_: E.dma_start(out=o, in_=i), "dw", 16)

    def dma_o(self, out, in_):
        return self.sig("sc", lambda E, o=out, i=in_: E.dma_start(out=o, in_=i), "do", 16)


def build_nc(debug_taps=(), upto="full", skip_ffn=False):
    nc = bass.Bass("TRN2", target_bir_lowering=False, detect_race_conditions=False)
    b = Builder(nc)

    def dram(n, s, ty, kind):
        return nc.dram_tensor(n, list(s), ty, kind=kind).ap()
    x_fm = dram("x_fm", (8, 128, L), F32, "ExternalInput")
    w_in = dram("w_in", (32, 128, 1024), F32, "ExternalInput")
    w_bc = dram("w_bc", (128, 512), F32, "ExternalInput")
    w_xpd = dram("w_xpd", (16, 128, 2048), F32, "ExternalInput")
    w_dtT = dram("w_dtT", (16, 128, 2048), F32, "ExternalInput")
    conv_w_d = dram("conv_w", (128, 64), F32, "ExternalInput")
    conv_b_d = dram("conv_b", (128, 16), F32, "ExternalInput")
    a_log_d = dram("a_log", (128, 256), F32, "ExternalInput")
    d_par_d = dram("d_par", (128, 16), F32, "ExternalInput")
    dt_b_d = dram("dt_b", (128, 16), F32, "ExternalInput")
    w_out = dram("w_out", (8, 128, 2048), F32, "ExternalInput")
    gate_w_d = dram("gate_w", (128, 32), F32, "ExternalInput")
    gate_b_d = dram("gate_b", (1, 4), F32, "ExternalInput")
    fw1 = dram("fw1", (NEXP, 32, 128, 1024), F32, "ExternalInput")
    fb1_d = dram("fb1", (128, NEXP * 32), F32, "ExternalInput")
    fw2 = dram("fw2", (NEXP, 8, 128, 4096), F32, "ExternalInput")
    fb2_d = dram("fb2", (128, NEXP * 8), F32, "ExternalInput")
    y_fm = dram("y_fm", (8, 128, L), F32, "ExternalOutput")
    mt2 = dram("mt2", (16, 128, 2048), BF16, "Internal")
    wxp_bfd = dram("wxp_bfd", (16, 128, 2048), BF16, "Internal")
    w_in_bf = dram("w_in_bf", (32, 128, 1024), BF16, "Internal")
    w_out_bf = dram("w_out_bf", (8, 128, 2048), BF16, "Internal")
    fw1_bf = dram("fw1_bf", (NEXP, 32, 128, 1024), BF16, "Internal")
    fw2_bf = dram("fw2_bf", (NEXP, 8, 128, 4096), BF16, "Internal")

    O = SB_BASE
    xh_t = b.sb("xh", 8 * 1024, F32, O); O += 32768

    def xslot(q):
        p = q % 2
        return [xh_t[:, k * 1024 + p * 512: k * 1024 + p * 512 + 512] for k in range(8)]
    xn_t = b.sb("xn", 8 * 1024, BF16, O); O += 16384

    def xnhalf(q):
        p = q % 2
        return [xn_t[:, k * 1024 + p * 512: k * 1024 + p * 512 + 512] for k in range(8)]

    def rwbc_for(q):
        p = (q + 1) % 2
        return [xn_t[:, e * 1024 + p * 512: e * 1024 + p * 512 + 512] for e in range(4)]

    O_BIGB = O
    bigb = b.sb("bigb", 32 * 1024, BF16, O_BIGB); O += 65536
    xi_act = [bigb[:, i * 512:(i + 1) * 512] for i in range(16)]
    sz = [bigb[:, (16 + i) * 512:(17 + i) * 512] for i in range(16)]
    HB = 16384
    hbuf = [bigb[:, HB + i * 512: HB + (i + 1) * 512] for i in range(32)]
    O_H2 = O_BIGB + 32768
    lnrow_m = b.sb("lnrow_m", 512, F32, O_H2)[0:1, :]
    lnrow_q = b.sb("lnrow_q", 512, F32, O_H2 + 2048)[0:1, :]
    lnrow_r = b.sb("lnrow_r", 512, F32, O_H2 + 4096)[0:1, :]
    rstd_bc = b.sb("rstd_bc", 512, F32, O_H2 + 6144)
    mr_bc = b.sb("mr_bc", 512, F32, O_H2 + 8192)
    lntmp = b.sb("lntmp", 512, F32, O_H2 + 10240)
    sqtmp = b.sb("sqtmp", 512, F32, O_H2 + 12288)
    cvt_f32 = [b.sb(f"cvt_f32_{i}", 2048, F32, O_H2 + i * 8192) for i in range(3)]
    cvt_bf = [b.sb(f"cvt_bf{i}", 2048, BF16, O_H2 + 24576 + i * 4096) for i in range(2)]
    cdtT = b.sb("cdtT", 16 * 512, F32, O_H2)
    O_GATED = O_BIGB + 65536
    gated_t = b.sb("gated", 8 * 1024, BF16, O_GATED)
    gated = [gated_t[:, i * 512:(i + 1) * 512] for i in range(16)]
    w2bf = [b.sb(f"w2bf{i}", 2048, BF16, O_GATED + 16384 + i * 4096) for i in range(2)]
    delta_r = [b.sb(f"delta{i}", 512, F32, O_GATED + 24576 + i * 2048) for i in range(2)]
    r0_b = b.sb("r0", 512, F32, O_GATED + 28672)
    r1_b = b.sb("r1", 512, F32, O_GATED + 30720)
    wxpf32 = [b.sb(f"wxpf32_{i}", 2048, F32, O_GATED + i * 8192) for i in range(2)]
    wxp_bf = [b.sb(f"wxp_bf{i}", 2048, BF16, O_GATED + 16384 + i * 4096) for i in range(2)]
    O_BBC = O_GATED + 32768
    bbc_t = b.sb("bbc", 32 * 512, BF16, O_BBC)
    bbc = [bbc_t[:, i * 512:(i + 1) * 512] for i in range(32)]
    cdtT_bf = b.sb("cdtT_bf", 16 * 512, BF16, O_BBC)
    O_WORK = O_BBC + 32768
    dx_b = b.sb("dx", 512, F32, O_WORK)
    BC_sb = b.sb("BC_sb", 512, BF16, O_WORK + 2048)[0:32, :]
    dA_b = b.sb("dA", 512, F32, O_WORK + 4096)
    vtmp = b.sb("vtmp", 512, F32, O_WORK + 6144)
    H_b = b.sb("H", 512, F32, O_WORK + 8192)
    y_acc = b.sb("y_acc", 512, F32, O_WORK + 10240)
    O_CONV = O_WORK + 12288
    xi_pre = b.sb("xi_pre", 528, F32, O_CONV)
    cacc = [b.sb(f"cacc{i}", 512, F32, O_CONV + 2112 + i * 2048) for i in range(2)]
    obf = b.sb("obf", 2048, BF16, O_CONV)
    wring = [b.sb(f"wring{i}", 2048, BF16, o) for i, o in enumerate(
        (O_CONV, O_CONV + 4096, O_GATED + 16384, O_GATED + 20480))]
    rwT = b.sb("rwT", 512, F32, O_CONV)[0:4, :]
    ffntmp = b.sb("ffntmp", 512, F32, O_WORK + 2048)
    w1bf = [b.sb(f"w1bf{i}", 1024, BF16, O_CONV + 6208 + i * 2048) for i in range(2)]
    dA1_b = b.sb("dA1", 512, F32, O_CONV + 8256)
    w4f32 = [b.sb(f"w4f32_{i}", 1024, F32, O_GATED + 24576 + i * 4096) for i in range(2)]
    wbf_in = [b.sb(f"wbf_in{i}", 1024, BF16, O_CONV + 6208 + i * 2048) for i in range(2)]
    O_MT = O_CONV + 10368
    mtb = b.sb("mtb", 2048, BF16, O_MT)
    iota_rbuf = b.sb("iota_rbuf", 128, F32, O_CONV + 2112)
    off = O_MT + 4096

    def small(name, cols, ty):
        nonlocal off
        ap = b.sb(name, cols, ty, off)
        itemsize = 2 if ty == BF16 else 4
        off = (off + cols * itemsize + 31) & ~31
        return ap

    A_sb = small("A_sb", 256, F32)
    conv_w = small("conv_w_s", 64, F32)
    conv_b = small("conv_b_s", 16, F32)
    d_par = small("d_par_s", 16, F32)
    dt_b = small("dt_b_s", 16, F32)
    state = small("state", 256, F32)
    carry = small("carry", 48, F32)
    fb1 = small("fb1_s", NEXP * 32, F32)
    fb2 = small("fb2_s", NEXP * 8, F32)
    fb2h = small("fb2h", 16, F32)
    gw_bf = small("gw_bf", 32, BF16)
    gb_f = small("gb_f", 4, F32)[0:1, :]
    gb_bf = small("gb_bf", 4, BF16)[0:1, :]
    wbc_bf = small("wbc_bf", 512, BF16)
    ones_col = small("ones_col", 1, F32)
    eps_col = small("eps_col", 1, F32)
    onesr_bf = small("onesr_bf", 128, BF16)[0:1, :]
    iota_c = small("iota_c", 1, F32)
    ident = small("ident", 128, F32)
    identbf = small("identbf", 32, BF16)
    onesr_f = ident[0:1, 0:1].broadcast_to([1, 128])
    assert off <= SB_LIMIT, off
    g16 = b.sb("g16", 16, F32, O_CONV + 4160)
    e16 = b.sb("e16", 16, F32, O_CONV + 4224)
    w16 = b.sb("w16", 16, F32, O_CONV + 4288)
    wm16 = b.sb("wm16", 16, F32, O_CONV + 4352)
    msk16 = b.sb("msk16", 16, F32, O_CONV + 4416)
    mx4 = b.sb("mx4", 4, F32, O_CONV + 4480)
    s4c = b.sb("s4c", 4, F32, O_CONV + 4512)
    m1c = b.sb("m1c", 4, F32, O_CONV + 4544)
    m2c = b.sb("m2c", 4, F32, O_CONV + 4576)
    gwf_st = b.sb("gwf_st", 32, F32, O_CONV + 4608)

    pb = [nc.alloc_psum_tensor(f"pb{i}", [128, 512], F32).ap() for i in range(8)]

    es = contextlib.ExitStack()
    for s in ("dw", "do", "te", "sc", "ve", "ge", "sy"):
        b.sems[s] = es.enter_context(nc.semaphore(s))

    for dst, src in ((conv_w, conv_w_d), (conv_b, conv_b_d), (A_sb, a_log_d),
                     (d_par, d_par_d), (dt_b, dt_b_d), (fb1, fb1_d), (fb2, fb2_d),
                     (gwf_st, gate_w_d), (gb_f, gate_b_d), (wbc_bf, w_bc)):
        if dst is wbc_bf:
            b.dma_w(cvt_f32[0][:, 0:512], src)
        else:
            b.dma_w(dst, src)
    dw_smalls = b.cnt["dw"]
    for ap, v in ((ones_col, 1.0), (eps_col, EPS), (onesr_bf, 1.0),
                  (state, 0.0)):
        b.run("ve", lambda E, ap=ap, v=v: E.memset(ap, v))
    b.run("ge", lambda E: E.iota(iota_c, [[0, 1]], base=0, channel_multiplier=1,
                                 allow_small_or_imprecise_dtypes=True))
    gdone = b.sig("ge", lambda E: E.iota(iota_rbuf, [[1, 128]], base=0, channel_multiplier=0,
                                         allow_small_or_imprecise_dtypes=True), "ge")
    b.wait("ve", "ge", gdone)
    b.run("ve", lambda E: E.tensor_scalar(out=ident, in0=iota_rbuf, scalar1=iota_c,
                                          scalar2=None, op0=ALU.is_equal))
    b.wait("sc", "dw", dw_smalls)
    exp_done = b.sig("sc", lambda E: E.activation(out=A_sb, in_=A_sb, func=AF.Exp,
                                                  bias=0.0, scale=1.0), "sc")
    b.wait("ve", "sc", exp_done)
    b.run("ve", lambda E: E.tensor_scalar_mul(out=A_sb, in0=A_sb, scalar1=-1.0))
    b.run("sc", lambda E: E.activation(out=wbc_bf, in_=cvt_f32[0][:, 0:512],
                                       func=AF.Identity, bias=0.0, scale=1.0))
    b.run("sc", lambda E: E.activation(out=gw_bf, in_=gwf_st, func=AF.Identity, bias=0.0, scale=1.0))
    b.run("sc", lambda E: E.activation(out=gb_bf, in_=gb_f, func=AF.Identity, bias=0.0, scale=1.0))
    b.run("sc", lambda E: E.activation(out=fb2h, in_=fb2[:, 0:16], func=AF.Identity, bias=0.0, scale=0.5))
    b.wait("sc", "ve", b.cnt["ve"])
    b.run("sc", lambda E: E.activation(out=identbf[0:32, :], in_=ident[0:32, 0:32],
                                       func=AF.Identity, bias=0.0, scale=1.0))
    b.barrier(("sy", "te", "sc", "ve", "ge"))

    gated_sig = {}
    scan_done = {0 - 1: 0}
    store_tgt = {}
    prep_sc_done = {}
    prep_te_done = {}
    prep_xn_done = {}
    tail_sc_done = {-1: 0, 0 - 2: 0}
    tail_te_done = {}
    tail_ve_done = {}
    dl_done = {}
    scansig = {}
    spill = {"w_in": 0, "w_out": 0, "ffn": 0}

    def emit_composition_gen():
        b.wait("sy", "ve", prep_xn_done[0])
        wxp_cast = {}
        comp_mm = {}
        comp_ev = {}
        scat = {}
        wsp = {}
        pend = []

        def flush_scat(upto):
            while pend and pend[0][0] <= upto:
                gg, oc_, it_, stage_ = pend.pop(0)
                b.wait("sy", "sc", comp_ev[gg])
                for j in range(4):
                    od = oc_ * 4 + j
                    scat[gg] = b.dma_w(mt2[od, :, it_ * 128:(it_ + 1) * 128],
                                       stage_[:, j * 128:(j + 1) * 128])
        g = 0
        for oc in range(4):
            dl = None
            for jb in range(16):
                dl = b.dma_w(cdtT[:, jb * 512:(jb + 1) * 512],
                             w_dtT[jb, :, oc * 512:(oc + 1) * 512])
            b.wait("sc", "dw", dl)
            for jb in range(16):
                b.run("sc", lambda E, jb=jb: E.activation(
                    out=cdtT_bf[:, jb * 512:(jb + 1) * 512],
                    in_=cdtT[:, jb * 512:(jb + 1) * 512], func=AF.Identity, bias=0.0, scale=1.0))
            b.nop_sig("sc", "sc")
            yield
            for it in range(16):
                slot = g % 2
                if oc == 0:
                    if g >= 2:
                        b.wait("sy", "sc", wxp_cast[g - 2])
                    d = b.dma_w(wxpf32[slot], w_xpd[it])
                    flush_scat(g - 2)
                    b.wait("sc", "dw", d)
                    if g >= 2:
                        b.wait("sc", "do", wsp[g - 2])
                        b.wait("sc", "dw", scat[g - 2])
                    wxp_cast[g] = b.sig("sc", lambda E, slot=slot: E.activation(
                        out=wxp_bf[slot], in_=wxpf32[slot], func=AF.Identity,
                        bias=0.0, scale=1.0), "sc")
                    b.wait("sc", "sc", wxp_cast[g])
                    wsp[g] = b.dma_o(wxp_bfd[it], wxp_bf[slot])
                    b.wait("te", "sc", wxp_cast[g])
                else:
                    if g >= 2:
                        b.wait("sy", "te", comp_mm[g - 2])
                    d = b.dma_w(wxp_bf[slot], wxp_bfd[it])
                    flush_scat(g - 2)
                    b.wait("te", "dw", d)
                bank = 4 + (g % 2)
                if g >= 2:
                    b.wait("te", "sc", comp_ev[g - 2])
                for jb in range(16):
                    mk = (lambda E, slot=slot, jb=jb, bank=bank, s=(jb == 0), p=(jb == 15):
                          E.matmul(out=pb[bank], lhsT=wxp_bf[slot][:, jb * 128:(jb + 1) * 128],
                                   rhs=cdtT_bf[:, jb * 512:(jb + 1) * 512], start=s, stop=p))
                    if jb == 15:
                        comp_mm[g] = b.sig("te", mk, "te")
                    else:
                        b.run("te", mk)
                b.wait("sc", "te", comp_mm[g])
                stage = mtb[:, 0:1024] if g % 2 == 0 else mtb[:, 1024:2048]
                if g >= 2:
                    b.wait("sc", "dw", scat[g - 2])
                comp_ev[g] = b.sig("sc", lambda E, bank=bank, stage=stage: E.activation(
                    out=stage[:, 0:512], in_=pb[bank], func=AF.Identity, bias=0.0, scale=1.0), "sc")
                pend.append((g, oc, it, stage))
                g += 1
                yield
        flush_scat(g)

    def emit_ln(xs, xns):
        for k in range(8):
            b.run("te", lambda E, k=k, s=(k == 0), p=(k == 7):
                  E.matmul(out=pb[0][0:1, :], lhsT=ones_col, rhs=xs[k], start=s, stop=p))
        mm_tgt = {}
        for k in range(8):
            if k >= 1:
                b.wait("sc", "te", mm_tgt[k - 1])
            sq = b.sig("sc", lambda E, k=k: E.activation(
                out=sqtmp, in_=xs[k], func=AF.Square, bias=0.0, scale=1.0), "sc")
            b.wait("te", "sc", sq)
            mm_tgt[k] = b.sig("te", lambda E, k=k, s=(k == 0), p=(k == 7):
                              E.matmul(out=pb[1][0:1, :], lhsT=ones_col,
                                       rhs=sqtmp, start=s, stop=p), "te")
        b.wait("ve", "te", mm_tgt[7])
        b.run("ve", lambda E: E.tensor_scalar_mul(out=lnrow_m, in0=pb[0][0:1, :], scalar1=1.0 / DM))
        b.run("ve", lambda E: E.tensor_scalar_mul(out=lnrow_q, in0=pb[1][0:1, :], scalar1=1.0 / DM))
        b.run("ve", lambda E: E.tensor_mul(out=lnrow_r, in0=lnrow_m, in1=lnrow_m))
        rows = b.sig("ve", lambda E: E.tensor_sub(out=lnrow_q, in0=lnrow_q, in1=lnrow_r), "ve")
        b.wait("sc", "ve", rows)
        rstd = b.sig("sc", lambda E: E.activation(out=lnrow_q, in_=lnrow_q, func=AF.Sqrt,
                                                  bias=eps_col[0:1, :], scale=1.0), "sc")
        b.wait("ve", "sc", rstd)
        b.run("ve", lambda E: E.reciprocal(out=lnrow_q, in_=lnrow_q))
        b.run("ve", lambda E: E.drain())
        mr = b.sig("ve", lambda E: E.tensor_mul(out=lnrow_r, in0=lnrow_m, in1=lnrow_q), "ve")
        b.wait("te", "ve", mr)
        b.run("te", lambda E: E.matmul(out=pb[2], lhsT=onesr_f, rhs=lnrow_q, start=True, stop=True))
        bc_mm = b.sig("te", lambda E: E.matmul(out=pb[3], lhsT=onesr_f, rhs=lnrow_r,
                                               start=True, stop=True), "te")
        b.wait("sc", "te", bc_mm)
        b.run("sc", lambda E: E.activation(out=rstd_bc, in_=pb[2], func=AF.Identity, bias=0.0, scale=1.0))
        ev = b.sig("sc", lambda E: E.activation(out=mr_bc, in_=pb[3], func=AF.Identity,
                                                bias=0.0, scale=1.0), "sc")
        b.wait("ve", "sc", ev)
        b.wait("te", "sc", ev)
        for k in range(8):
            b.run("ve", lambda E, k=k: E.tensor_mul(out=lntmp, in0=xs[k], in1=rstd_bc))
            b.run("ve", lambda E, k=k, xns=xns: E.tensor_sub(out=xns[k], in0=lntmp, in1=mr_bc))
        return b.nop_sig("ve", "ve")

    def emit_prep(q, comp_gen=None):
        xs = xslot(q)
        xns = xnhalf(q)
        if q >= 2:
            b.wait("sy", "do", store_tgt[q - 2])
        if q == 1:
            b.wait("sy", "do", spill["w_in"])
        if q >= 1:
            b.wait("sy", "te", b.cnt["te"])
            b.wait("sy", "ve", scan_done.get(q - 1, 0))
        b.wait("te", "sc", tail_sc_done.get(q - 2, 0))
        b.wait("te", "ve", tail_ve_done.get(q - 2, 0))
        b.wait("sc", "te", tail_te_done.get(q - 2, 0))
        xd = None
        for k in range(8):
            xd = b.dma_w(xs[k], x_fm[k, :, q * 512:(q + 1) * 512])
        for e in ("te", "sc", "ve"):
            b.wait(e, "dw", xd)
        prep_xn_done[q] = emit_ln(xs, xns)
        b.wait("te", "ve", prep_xn_done[q])

        cast_tgt = {}
        evac_sc = {}
        for mt in range(32):
            slot = mt % 2
            if q == 0:
                if mt >= 2:
                    b.wait("sy", "sc", cast_tgt[mt - 2])
                d = b.dma_w(w4f32[slot], w_in[mt])
                b.wait("sc", "dw", d)
                if mt >= 2:
                    b.wait("sc", "do", inspill[mt - 2])
                cast_tgt[mt] = b.sig("sc", lambda E, slot=slot: E.activation(
                    out=wbf_in[slot], in_=w4f32[slot], func=AF.Identity, bias=0.0, scale=1.0), "sc")
                b.wait("sc", "sc", cast_tgt[mt])
                inspill[mt] = b.dma_o(w_in_bf[mt], wbf_in[slot])
                spill["w_in"] = inspill[mt]
                b.wait("te", "sc", cast_tgt[mt])
            else:
                if mt >= 2:
                    b.wait("sy", "te", mm_prev[mt - 2])
                d = b.dma_w(wbf_in[slot], w_in_bf[mt])
                b.wait("te", "dw", d)
            bank = mt % 4
            if mt >= 4:
                b.wait("te", "sc", evac_sc[mt - 4])
            for k in range(8):
                mk = (lambda E, slot=slot, k=k, bank=bank, s=(k == 0), p=(k == 7):
                      E.matmul(out=pb[bank], lhsT=wbf_in[slot][:, k * 128:(k + 1) * 128],
                               rhs=xns[k], start=s, stop=p))
                if k == 7:
                    mm = b.sig("te", mk, "te")
                else:
                    b.run("te", mk)
            mm_prev[mt] = mm
            b.wait("sc", "te", mm)
            if mt < 16:
                dblk = mt
                if q >= 1:
                    b.wait("sc", "ve", gated_sig[(q - 1, dblk)])
                b.run("sc", lambda E, bank=bank: E.activation(
                    out=xi_pre[:, 3:515], in_=pb[bank], func=AF.Identity, bias=0.0, scale=1.0))
                if q == 0:
                    b.run("sc", lambda E: E.memzero(xi_pre[:, 0:3]))
                else:
                    b.run("sc", lambda E, dblk=dblk: E.activation(
                        out=xi_pre[:, 0:3], in_=carry[:, dblk * 3:(dblk + 1) * 3],
                        func=AF.Identity, bias=0.0, scale=1.0))
                b.run("sc", lambda E, dblk=dblk: E.activation(
                    out=carry[:, dblk * 3:(dblk + 1) * 3], in_=xi_pre[:, 512:515],
                    func=AF.Identity, bias=0.0, scale=1.0))
                tap0 = b.sig("sc", lambda E, dblk=dblk: E.activation(
                    out=cacc[0], in_=xi_pre[:, 0:512], func=AF.Identity,
                    bias=conv_b[:, dblk:dblk + 1], scale=conv_w[:, dblk * 4:dblk * 4 + 1]), "sc")
                b.wait("ve", "sc", tap0)
                for t in range(1, 3):
                    b.run("ve", lambda E, dblk=dblk, t=t: E.scalar_tensor_tensor(
                        out=cacc[t % 2], in0=xi_pre[:, t:t + 512],
                        scalar=conv_w[:, dblk * 4 + t:dblk * 4 + t + 1],
                        in1=cacc[(t - 1) % 2], op0=ALU.mult, op1=ALU.add))
                taps = b.sig("ve", lambda E, dblk=dblk: E.scalar_tensor_tensor(
                    out=cacc[1], in0=xi_pre[:, 3:515],
                    scalar=conv_w[:, dblk * 4 + 3:dblk * 4 + 4],
                    in1=cacc[0], op0=ALU.mult, op1=ALU.add), "ve")
                b.wait("sc", "ve", taps)
                evac_sc[mt] = b.sig("sc", lambda E, dblk=dblk: E.activation(
                    out=xi_act[dblk], in_=cacc[1], func=AF.Silu, bias=0.0, scale=1.0), "sc")
            else:
                dblk = mt - 16
                if q >= 1:
                    b.wait("sc", "ve", gated_sig[(q - 1, dblk)])
                evac_sc[mt] = b.sig("sc", lambda E, bank=bank, dblk=dblk: E.activation(
                    out=sz[dblk], in_=pb[bank], func=AF.Silu, bias=0.0, scale=1.0), "sc")
            if comp_gen is not None:
                next(comp_gen, None)
                next(comp_gen, None)
        if comp_gen is not None:
            for _ in comp_gen:
                pass
            b.wait("te", "sc", b.cnt["sc"])
            b.wait("sy", "te", b.cnt["te"])
            b.wait("sy", "sc", b.cnt["sc"])
            b.wait("sy", "do", b.cnt["do"])
        b.wait("te", "sc", evac_sc[31])
        for k in range(16):
            mk = (lambda E, k=k, s=(k == 0), p=(k == 15):
                  E.matmul(out=pb[4][0:32, :], lhsT=wbc_bf[:, k * 32:(k + 1) * 32],
                           rhs=xi_act[k], start=s, stop=p))
            if k == 15:
                mm = b.sig("te", mk, "te")
            else:
                b.run("te", mk)
        b.wait("sc", "te", mm)
        bc_ev = b.sig("sc", lambda E: E.activation(
            out=BC_sb, in_=pb[4][0:32, :], func=AF.Identity, bias=0.0, scale=1.0), "sc")
        b.wait("te", "sc", bc_ev)
        prep_sc_done[q] = b.cnt["sc"]
        prep_te_done[q] = b.cnt["te"]

    inspill = {}
    mm_prev = {}

    def emit_bcasts(q):
        b.wait("sc", "ve", scan_done[q - 1])
        b.wait("te", "ve", scan_done[q - 1])
        evs = {}
        for r in range(32):
            bank = 2 + (r % 6)
            if r >= 6:
                b.wait("te", "sc", evs[r - 6])
            sel = identbf[0:32, r:r + 1].broadcast_to([32, 128])
            mm = b.sig("te", lambda E, sel=sel, bank=bank: E.matmul(
                out=pb[bank], lhsT=sel, rhs=BC_sb, start=True, stop=True), "te")
            b.wait("sc", "te", mm)
            evs[r] = b.sig("sc", lambda E, r=r, bank=bank: E.activation(
                out=bbc[r], in_=pb[bank], func=AF.Identity, bias=0.0, scale=1.0), "sc")
        b.wait("te", "sc", evs[31])
        b.wait("te", "sc", evs[30])

    def emit_m_od(q, od, prev_mm=0):
        b.wait("sy", "te", prev_mm)
        d = b.dma_w(mtb, mt2[od, :, :])
        b.wait("te", "dw", d)
        bank = 6 + (od % 2)
        for ib in range(16):
            mk = (lambda E, ib=ib, bank=bank, s=(ib == 0), p=(ib == 15):
                  E.matmul(out=pb[bank], lhsT=mtb[:, ib * 128:(ib + 1) * 128],
                           rhs=xi_act[ib], start=s, stop=p))
            if ib == 15:
                mm = b.sig("te", mk, "te")
            else:
                b.run("te", mk)
        return mm

    def emit_softplus(q, od, m_mm):
        if od >= 1:
            b.wait("sc", "ve", dl_done[(q, od - 1)])
        b.wait("sc", "te", m_mm)
        bank = 6 + (od % 2)
        b.run("sc", lambda E, bank=bank, od=od: E.activation(
            out=r0_b, in_=pb[bank], func=AF.Identity,
            bias=dt_b[:, od:od + 1], scale=1.0))
        b.run("sc", lambda E: E.activation(out=r1_b, in_=r0_b, func=AF.Abs, bias=0.0, scale=1.0))
        b.run("sc", lambda E: E.activation(out=r1_b, in_=r1_b, func=AF.Exp, bias=0.0, scale=-1.0))
        return b.sig("sc", lambda E: E.activation(out=r1_b, in_=r1_b, func=AF.Ln,
                                                  bias=ones_col, scale=1.0), "sc")

    def emit_scan(q, tail_gen, aux_gen=None):
        def pump(k):
            if tail_gen is not None:
                for _ in range(k):
                    if next(tail_gen, "END") == "END":
                        break
        def pump_aux(k):
            if aux_gen is not None:
                for _ in range(k):
                    if next(aux_gen, "END") == "END":
                        break
        m_mm = {}
        sp_sig = {}
        b.wait("sc", "te", prep_te_done.get(q, 0))
        if q == 0:
            b.wait("sc", "do", spill["w_in"])
        m_mm[0] = emit_m_od(q, 0)
        sp_sig[0] = emit_softplus(q, 0, m_mm[0])
        m_mm[1] = emit_m_od(q, 1)
        for od in range(16):
            dblk = od
            ds = od % 2
            dlt = delta_r[ds]
            sp = sp_sig[od]
            b.wait("ve", "sc", sp)
            b.run("ve", lambda E, dlt=dlt: E.scalar_tensor_tensor(
                out=dlt, in0=r0_b, scalar=0.0, in1=r1_b, op0=ALU.max, op1=ALU.add))
            dl = b.sig("ve", lambda E, dlt=dlt, dblk=dblk: E.tensor_mul(
                out=dx_b, in0=dlt, in1=xi_act[dblk]), "ve")
            dl_done[(q, od)] = dl
            if od + 1 < 16:
                sp_sig[od + 1] = emit_softplus(q, od + 1, m_mm[od + 1])
            if od + 2 < 16:
                b.wait("te", "sc", sp_sig[od])
                m_mm[od + 2] = emit_m_od(q, od + 2, m_mm[od + 1])
            post = None
            save_state = q < NQ - 1
            for n in range(16):
                col = dblk * 16 + n
                gn = q * 256 + dblk * 16 + n
                dAs = dA_b if n % 2 == 0 else dA1_b
                b.wait("sc", "ve", scansig.get(gn - 2, 0))
                da = b.sig("sc", lambda E, dlt=dlt, col=col, dAs=dAs: E.activation(
                    out=dAs, in_=dlt, func=AF.Exp, bias=0.0,
                    scale=A_sb[:, col:col + 1]), "sc")
                b.run("ve", lambda E, n=n: E.tensor_mul(out=vtmp, in0=dx_b, in1=bbc[n]))
                b.wait("ve", "sc", da)
                scansig[gn] = b.sig("ve", lambda E, col=col, dAs=dAs: E.tensor_tensor_scan(
                    out=H_b, data0=dAs, data1=vtmp,
                    initial=state[:, col:col + 1], op0=ALU.mult, op1=ALU.add), "ve")
                if save_state:
                    b.run("ve", lambda E, col=col: E.tensor_copy(
                        out=state[:, col:col + 1], in_=H_b[:, 511:512]))
                if n == 0:
                    post = b.sig("ve", lambda E, n=n: E.tensor_mul(
                        out=y_acc, in0=H_b, in1=bbc[16 + n]), "ve")
                else:
                    b.run("ve", lambda E, n=n: E.tensor_mul(out=vtmp, in0=H_b, in1=bbc[16 + n]))
                    post = b.sig("ve", lambda E: E.tensor_add(
                        out=y_acc, in0=y_acc, in1=vtmp), "ve")
                if n % 2 == 1:
                    pump(1)
                pump_aux(1)
            b.run("ve", lambda E, dblk=dblk: E.scalar_tensor_tensor(
                out=y_acc, in0=xi_act[dblk], scalar=d_par[:, dblk:dblk + 1],
                in1=y_acc, op0=ALU.mult, op1=ALU.add))
            if q >= 1 and dblk == 0:
                while (q - 1) not in outproj_mm_last:
                    pump(1)
                b.wait("ve", "te", outproj_mm_last[q - 1])
            gated_sig[(q, dblk)] = b.sig("ve", lambda E, dblk=dblk: E.tensor_mul(
                out=gated[dblk], in0=y_acc, in1=sz[dblk]), "ve")
            pump(2)
            pump_aux(4)
        scan_done[q] = b.cnt["ve"]
        if tail_gen is not None:
            for _ in tail_gen:
                pass
        if aux_gen is not None:
            for _ in aux_gen:
                pass

    outproj_mm_last = {}

    def emit_conversion_gen():
        jobs = [("w_out", w_out[mo], w_out_bf[mo], 2048, 1.0) for mo in range(8)]
        for e in range(NEXP):
            for mt in range(32):
                jobs.append(("ffn", fw1[e, mt], fw1_bf[e, mt], 1024, 1.0))
        for e in range(NEXP):
            sc2 = 0.5 if e < 2 else 1.0
            for mo in range(8):
                jobs.append(("ffn", fw2[e, mo, :, 0:2048], fw2_bf[e, mo, :, 0:2048], 2048, sc2))
                jobs.append(("ffn", fw2[e, mo, :, 2048:4096], fw2_bf[e, mo, :, 2048:4096], 2048, sc2))
        casts = {}
        spills = {}
        for j, (kind, src, dst, cols, scl) in enumerate(jobs):
            slot = j % 3
            bslot = j % 2
            if j >= 3:
                b.wait("sy", "sc", casts[j - 3])
            d = b.dma_w(cvt_f32[slot][:, 0:cols], src)
            b.wait("sc", "dw", d)
            if j >= 2:
                b.wait("sc", "do", spills[j - 2])
            casts[j] = b.sig("sc", lambda E, slot=slot, bslot=bslot, cols=cols, scl=scl: E.activation(
                out=cvt_bf[bslot][:, 0:cols], in_=cvt_f32[slot][:, 0:cols],
                func=AF.Identity, bias=0.0, scale=scl), "sc")
            b.wait("sc", "sc", casts[j])
            spills[j] = b.dma_o(dst, cvt_bf[bslot][:, 0:cols])
            spill[kind] = spills[j]
            yield

    def emit_tail_gen(q):
        xs = xslot(q)
        xns = xnhalf(q)
        rwbc = rwbc_for(q)
        b.wait("te", "ve", scan_done.get(q, 0))
        b.wait("te", "sc", prep_sc_done.get(q + 1, 0))
        b.wait("te", "sc", tail_sc_done.get(q - 1, 0))
        b.wait("te", "ve", prep_xn_done.get(q + 1, 0))
        if q == 0:
            b.wait("sy", "do", spill["w_out"])
        b.wait("sy", "sc", prep_sc_done.get(q + 1, 0))
        b.wait("sy", "te", max(prep_te_done.get(q + 1, 0), tail_te_done.get(q - 1, 0)))
        b.wait("sy", "ve", tail_ve_done.get(q - 1, 0))
        op_evs = {}
        mm_last = {}
        ring_user = {}
        for mo in range(8):
            slot = mo % 4
            if slot in ring_user:
                b.wait("sy", "te", ring_user[slot])
            d = b.dma_w(wring[slot], w_out_bf[mo])
            if mo >= 1:
                b.wait("ve", "te", mm_last[mo - 1])
                op_evs[mo - 1] = b.sig("ve", lambda E, mo=mo: E.tensor_add(
                    out=xs[mo - 1], in0=pb[(mo - 1) % 4], in1=xs[mo - 1]), "ve")
            b.wait("te", "dw", d)
            bank = mo % 4
            if mo >= 4:
                b.wait("te", "ve", op_evs[mo - 4])
            for k in range(16):
                mk = (lambda E, slot=slot, k=k, bank=bank, s=(k == 0), p=(k == 15):
                      E.matmul(out=pb[bank], lhsT=wring[slot][:, k * 128:(k + 1) * 128],
                               rhs=gated[k], start=s, stop=p))
                if k == 15:
                    mm = b.sig("te", mk, "te")
                else:
                    b.run("te", mk)
            mm_last[mo] = mm
            ring_user[slot] = mm
            yield
        b.wait("ve", "te", mm_last[7])
        op_evs[7] = b.sig("ve", lambda E: E.tensor_add(
            out=xs[7], in0=pb[3], in1=xs[7]), "ve")
        outproj_mm_last[q] = mm_last[7]
        b.wait("te", "ve", op_evs[7])
        b.wait("sc", "ve", op_evs[7])
        xn_done = emit_ln(xs, xns)
        yield
        b.wait("te", "ve", xn_done)
        for tg in range(4):
            for k in range(8):
                b.run("te", lambda E, tg=tg, k=k, s=(k == 0):
                      E.matmul(out=pb[0][:, tg * 4:(tg + 1) * 4],
                               lhsT=xns[k][:, tg * 128:(tg + 1) * 128],
                               rhs=gw_bf[:, k * 4:(k + 1) * 4], start=s, stop=False))
            mk = (lambda E, tg=tg: E.matmul(
                out=pb[0][:, tg * 4:(tg + 1) * 4], lhsT=onesr_bf, rhs=gb_bf,
                start=False, stop=True))
            if tg == 3:
                mm = b.sig("te", mk, "te")
            else:
                b.run("te", mk)
        b.wait("sc", "te", mm)
        evg = b.sig("sc", lambda E: E.activation(
            out=g16, in_=pb[0][:, 0:16], func=AF.Identity, bias=0.0, scale=1.0), "sc")
        b.wait("ve", "sc", evg)

        def rv(mk):
            b.run("ve", mk)
            b.run("ve", lambda E: E.drain())
        g3 = g16.rearrange('p (a b) -> p a b', a=4)
        e3 = e16.rearrange('p (a b) -> p a b', a=4)
        w3 = w16.rearrange('p (a b) -> p a b', a=4)
        wm3 = wm16.rearrange('p (a b) -> p a b', a=4)
        mk3 = msk16.rearrange('p (a b) -> p a b', a=4)
        bc4 = lambda a: a.unsqueeze(-1).broadcast_to([128, 4, 4])
        rv(lambda E: E.reduce_max(out=mx4, in_=g3, axis=AX.X))
        rv(lambda E: E.tensor_sub(out=e3, in0=g3, in1=bc4(mx4)))
        mx = b.nop_sig("ve", "ve")
        b.wait("sc", "ve", mx)
        ex = b.sig("sc", lambda E: E.activation(
            out=e16, in_=e16, func=AF.Exp, bias=0.0, scale=1.0), "sc")
        b.wait("ve", "sc", ex)
        rv(lambda E: E.reduce_sum(out=s4c, in_=e3, axis=AX.X))
        rv(lambda E: E.reciprocal(out=s4c, in_=s4c))
        rv(lambda E: E.tensor_mul(out=w3, in0=e3, in1=bc4(s4c)))
        rv(lambda E: E.reduce_max(out=m1c, in_=w3, axis=AX.X))
        rv(lambda E: E.tensor_tensor(out=mk3, in0=w3, in1=bc4(m1c), op=ALU.is_ge))
        rv(lambda E: E.scalar_tensor_tensor(out=wm3, in0=mk3, scalar=-1e30,
                                            in1=w3, op0=ALU.mult, op1=ALU.add))
        rv(lambda E: E.reduce_max(out=m2c, in_=wm3, axis=AX.X))
        rv(lambda E: E.tensor_tensor(out=mk3, in0=w3, in1=bc4(m2c), op=ALU.is_ge))
        rv(lambda E: E.tensor_mul(out=w3, in0=w3, in1=mk3))
        rw = b.nop_sig("ve", "ve")
        b.wait("te", "ve", rw)
        yield
        evt = {}
        for tg in range(4):
            bank2 = 2 + (tg % 2)
            if tg >= 2:
                b.wait("te", "sc", evt[tg - 2])
            tr = b.sig("te", lambda E, tg=tg, bank2=bank2: E.transpose(
                out=pb[bank2][0:4, 0:128], in_=w16[:, tg * 4:(tg + 1) * 4],
                identity=ident), "te")
            b.wait("sc", "te", tr)
            evt[tg] = b.sig("sc", lambda E, tg=tg, bank2=bank2: E.activation(
                out=rwT[:, tg * 128:(tg + 1) * 128], in_=pb[bank2][0:4, 0:128],
                func=AF.Identity, bias=0.0, scale=1.0), "sc")
        b.wait("te", "sc", evt[3])
        b.wait("te", "sc", evt[2])
        b.wait("ve", "sc", evt[3])
        yield
        revs = {}
        for e4 in range(4):
            bank = 4 + (e4 % 2)
            if e4 >= 2:
                b.wait("te", "sc", revs[e4 - 2])
            sel = ident[0:4, e4:e4 + 1].broadcast_to([4, 128])
            mm = b.sig("te", lambda E, sel=sel, bank=bank: E.matmul(
                out=pb[bank], lhsT=sel, rhs=rwT, start=True, stop=True), "te")
            b.wait("sc", "te", mm)
            revs[e4] = b.sig("sc", lambda E, e4=e4, bank=bank, rwbc=rwbc: E.activation(
                out=rwbc[e4], in_=pb[bank], func=AF.Identity, bias=0.0, scale=1.0), "sc")
        b.wait("te", "sc", revs[3])
        yield
        if skip_ffn:
            b.wait("sc", "ve", op_evs[7])
            st = None
            for k in range(8):
                st = b.dma_o(y_fm[k, :, q * 512:(q + 1) * 512], xs[k])
            store_tgt[q] = st
            tail_sc_done[q] = b.cnt["sc"]
            tail_te_done[q] = b.cnt["te"]
            tail_ve_done[q] = b.cnt["ve"]
            yield
            return
        if q == 0:
            b.wait("sy", "do", spill["ffn"])
        b.wait("sy", "te", b.cnt["te"])
        dn_ev = {}
        ring_user = {}
        rslot = [0]

        def ring_load(srcs):
            slot = rslot[0] % 4
            rslot[0] += 1
            if slot in ring_user:
                b.wait("sy", "te", ring_user[slot])
            n = 2048 // len(srcs)
            d = None
            for i, src in enumerate(srcs):
                d = b.dma_w(wring[slot][:, i * n:(i + 1) * n], src)
            return slot, d

        def up_evac(e, idx):
            b.wait("sc", "te", up_mm[idx])
            up_ev[idx] = b.sig("sc", lambda E, e=e, idx=idx: E.activation(
                out=hbuf[idx], in_=pb[idx % 2], func=AF.Relu,
                bias=fb1[:, e * 32 + idx:e * 32 + idx + 1], scale=1.0), "sc")

        for e in range(NEXP):
            up_mm = {}
            up_ev = {}
            for mt2_ in range(16):
                mt = 2 * mt2_
                slot, d = ring_load([fw1_bf[e, mt], fw1_bf[e, mt + 1]])
                if mt2_ >= 1:
                    up_evac(e, mt - 2)
                    up_evac(e, mt - 1)
                b.wait("te", "dw", d)
                for j in range(2):
                    bank = (mt + j) % 2
                    if mt + j >= 2:
                        b.wait("te", "sc", up_ev[mt + j - 2])
                    for k in range(8):
                        mk = (lambda E, slot=slot, j=j, k=k, bank=bank, s=(k == 0), p=(k == 7):
                              E.matmul(out=pb[bank],
                                       lhsT=wring[slot][:, j * 1024 + k * 128: j * 1024 + (k + 1) * 128],
                                       rhs=xns[k], start=s, stop=p))
                        if k == 7:
                            mm = b.sig("te", mk, "te")
                        else:
                            b.run("te", mk)
                    up_mm[mt + j] = mm
                ring_user[slot] = up_mm[mt + 1]
                yield
            up_evac(e, 30)
            up_evac(e, 31)
            b.wait("te", "sc", up_ev[31])
            def dn_evac(e, mo):
                bank = 2 + (mo % 4)
                b.wait("ve", "te", dn_mm[2 * mo + 1])
                if e < 2:
                    ev = b.sig("ve", lambda E, bank=bank, e=e, mo=mo: E.scalar_tensor_tensor(
                        out=xs[mo], in0=pb[bank],
                        scalar=fb2h[:, e * 8 + mo:e * 8 + mo + 1],
                        in1=xs[mo], op0=ALU.add, op1=ALU.add), "ve")
                else:
                    b.run("ve", lambda E, bank=bank, e=e, mo=mo, rwbc=rwbc: E.scalar_tensor_tensor(
                        out=ffntmp, in0=pb[bank],
                        scalar=fb2[:, e * 8 + mo:e * 8 + mo + 1],
                        in1=rwbc[e - 2], op0=ALU.add, op1=ALU.mult))
                    ev = b.sig("ve", lambda E, mo=mo: E.tensor_add(
                        out=xs[mo], in0=xs[mo], in1=ffntmp), "ve")
                dn_ev[(e, mo)] = ev

            dn_mm = {}
            for mo in range(8):
                for half in range(2):
                    g2 = 2 * mo + half
                    slot, d = ring_load([fw2_bf[e, mo, :, half * 2048:(half + 1) * 2048]])
                    if half == 0 and mo >= 1:
                        dn_evac(e, mo - 1)
                    b.wait("te", "dw", d)
                    bank = 2 + (mo % 4)
                    if half == 0 and mo >= 4:
                        b.wait("te", "ve", dn_ev[(e, mo - 4)])
                    for k in range(16):
                        mk = (lambda E, slot=slot, k=k, bank=bank, half=half,
                              s=(half == 0 and k == 0), p=(half == 1 and k == 15):
                              E.matmul(out=pb[bank], lhsT=wring[slot][:, k * 128:(k + 1) * 128],
                                       rhs=hbuf[half * 16 + k], start=s, stop=p))
                        if k == 15:
                            mm = b.sig("te", mk, "te")
                        else:
                            b.run("te", mk)
                    dn_mm[g2] = mm
                    ring_user[slot] = mm
                yield
            dn_evac(e, 7)
        b.wait("sc", "ve", dn_ev[(NEXP - 1, 7)])
        st = None
        for k in range(8):
            st = b.dma_o(y_fm[k, :, q * 512:(q + 1) * 512], xs[k])
        store_tgt[q] = st
        tail_sc_done[q] = b.cnt["sc"]
        tail_te_done[q] = b.cnt["te"]
        tail_ve_done[q] = b.cnt["ve"]
        yield

    for q in range(NQ + 1):
        if q < NQ:
            emit_prep(q, comp_gen=emit_composition_gen() if q == 0 else None)
            emit_bcasts(q)
        tail_gen = emit_tail_gen(q - 1) if q >= 1 else None
        if q < NQ:
            emit_scan(q, tail_gen, emit_conversion_gen() if q == 0 else None)
        else:
            if tail_gen is not None:
                for _ in tail_gen:
                    pass

    tapmap = {"xn": xn_t, "bigb": bigb, "BCfull": BC_sb, "gated": gated_t,
              "xh": xh_t, "state": state, "bbcT": bbc_t}
    out_tgts = [store_tgt[qq] for qq in store_tgt]
    for name in debug_taps:
        ap = tapmap[name]
        tap = nc.dram_tensor(name, list(ap.shape), ap.dtype, kind="ExternalOutput").ap()
        b.barrier()
        out_tgts.append(b.dma_o(tap, ap))

    if out_tgts:
        b.wait("sy", "do", max(out_tgts))
    b.wait("sy", "dw", b.cnt["dw"])

    with nc.Block() as blk:
        @blk.sync
        def _(E):
            for fn in b.cmds["sy"]:
                fn(E)

        @blk.tensor
        def _(E):
            for fn in b.cmds["te"]:
                fn(E)

        @blk.scalar
        def _(E):
            for fn in b.cmds["sc"]:
                fn(E)

        @blk.vector
        def _(E):
            for fn in b.cmds["ve"]:
                fn(E)

        @blk.gpsimd
        def _(E):
            for fn in b.cmds["ge"]:
                fn(E)

    es.close()
    return nc, b


def prep_inputs(L0_attn_x, L0_attn_in_proj_w, L0_attn_conv_w, L0_attn_conv_b,
                L0_attn_x_proj_w, L0_attn_dt_proj_w, L0_attn_dt_proj_b, L0_attn_A_log,
                L0_attn_D_param, L0_attn_out_proj_w, L0_ffn_gate_w, L0_ffn_gate_b,
                L0_ffn_sw1, L0_ffn_sb1, L0_ffn_sw2, L0_ffn_sb2, L0_ffn_rw1, L0_ffn_rb1,
                L0_ffn_rw2, L0_ffn_rb2):
    f = np.ascontiguousarray
    W1 = np.asarray(L0_attn_in_proj_w, np.float32)
    w_in = f(W1.reshape(32, 128, 8, 128).transpose(0, 3, 2, 1)).reshape(32, 128, 1024)
    WBC = np.asarray(L0_attn_x_proj_w, np.float32)[2048:2080]
    w_bc = f(WBC.reshape(32, 16, 128).transpose(2, 1, 0)).reshape(128, 512)
    WXD = np.asarray(L0_attn_x_proj_w, np.float32)[0:2048]
    w_xpd = f(WXD.reshape(16, 128, 16, 128).transpose(2, 1, 0, 3)).reshape(16, 128, 2048)
    w_dtT = f(np.asarray(L0_attn_dt_proj_w, np.float32).T).reshape(16, 128, 2048)
    conv_w = f(np.asarray(L0_attn_conv_w, np.float32).reshape(16, 128, 4)
               .transpose(1, 0, 2)).reshape(128, 64)
    conv_b = f(np.asarray(L0_attn_conv_b, np.float32).reshape(16, 128).T)
    a_log = f(np.asarray(L0_attn_A_log, np.float32).reshape(16, 128, 16)
              .transpose(1, 0, 2)).reshape(128, 256)
    d_par = f(np.asarray(L0_attn_D_param, np.float32).reshape(16, 128).T)
    dt_b = f(np.asarray(L0_attn_dt_proj_b, np.float32).reshape(16, 128).T)
    WO = np.asarray(L0_attn_out_proj_w, np.float32)
    w_out = f(WO.reshape(8, 128, 16, 128).transpose(0, 3, 2, 1)).reshape(8, 128, 2048)
    gate_w = f(np.asarray(L0_ffn_gate_w, np.float32).reshape(8, 128, 4)
               .transpose(1, 0, 2)).reshape(128, 32)
    gate_b = np.asarray(L0_ffn_gate_b, np.float32).reshape(1, 4)

    def up_bundle(w):
        return f(w.reshape(8, 128, 32, 128).transpose(2, 1, 0, 3)).reshape(32, 128, 1024)

    def down_bundle(w):
        return f(w.reshape(32, 128, 8, 128).transpose(2, 1, 0, 3)).reshape(8, 128, 4096)

    sw1 = np.asarray(L0_ffn_sw1, np.float32)
    rw1 = np.asarray(L0_ffn_rw1, np.float32)
    sw2 = np.asarray(L0_ffn_sw2, np.float32)
    rw2 = np.asarray(L0_ffn_rw2, np.float32)
    fw1 = np.stack([up_bundle(sw1[0]), up_bundle(sw1[1])] +
                   [up_bundle(rw1[i]) for i in range(4)])
    fw2 = np.stack([down_bundle(sw2[0]), down_bundle(sw2[1])] +
                   [down_bundle(rw2[i]) for i in range(4)])
    sb1 = np.asarray(L0_ffn_sb1, np.float32)
    rb1 = np.asarray(L0_ffn_rb1, np.float32)
    sb2 = np.asarray(L0_ffn_sb2, np.float32)
    rb2 = np.asarray(L0_ffn_rb2, np.float32)
    fb1 = f(np.concatenate([sb1, rb1], 0).reshape(6, 32, 128)
            .transpose(2, 0, 1)).reshape(128, 192)
    fb2 = f(np.concatenate([sb2, rb2], 0).reshape(6, 8, 128)
            .transpose(2, 0, 1)).reshape(128, 48)

    shared = dict(w_in=w_in, w_bc=w_bc, w_xpd=w_xpd, w_dtT=w_dtT, conv_w=conv_w,
                  conv_b=conv_b, a_log=a_log, d_par=d_par, dt_b=dt_b, w_out=w_out,
                  gate_w=gate_w, gate_b=gate_b, fw1=fw1, fb1=fb1, fw2=fw2, fb2=fb2)

    x = np.asarray(L0_attn_x, np.float32)
    in_maps = []
    for bi in range(8):
        x_fm_b = f(x[bi].T).reshape(8, 128, 2048)
        in_maps.append(dict(shared, x_fm=x_fm_b))
    return in_maps



def kernel(**inputs):
    in_maps = prep_inputs(**inputs)
    nc, _ = build_nc()
    res = run_bass_kernel_spmd(nc, in_maps, core_ids=list(range(8)))
    out = np.empty((8, 2048, 1024), np.float32)
    for bi in range(8):
        out[bi] = res.results[bi]["y_fm"].reshape(1024, 2048).T
    return out
